# version 09
# baseline (speedup 1.0000x reference)
import numpy as np
import ml_dtypes
import concourse.bass as bass
import concourse.mybir as mybir
from concourse.bass_utils import run_bass_kernel_spmd
from contextlib import ExitStack

F32 = mybir.dt.float32
BF16 = mybir.dt.bfloat16
AF = mybir.ActivationFunctionType
ALU = mybir.AluOpType
AX = mybir.AxisListType

NB = 2
S_LEN = 2048
D = 1024
DC = 8
NT = 4
TT = 512
HH = 4
NE = 10
NPAIR = 8
NSB = 16
EPS = 1e-5


def build_program(limit=10 ** 9):
    import itertools
    nc = bass.Bass("TRN2", target_bir_lowering=False, detect_race_conditions=False)
    _uid = itertools.count()

    def sbuf_t(name, shape, dtype):
        return nc.sbuf_tensor(f"{name}_{next(_uid)}", shape, dtype)

    def psum_t(name, shape, dtype=F32):
        return nc.psum_tensor(f"{name}_{next(_uid)}", shape, dtype)

    def sem_t(name):
        return nc.semaphore(f"{name}_{next(_uid)}")

    _NO_DRAIN = {"dma_start", "nop", "wait_ge", "sem_clear", "sem_inc", "drain",
                 "then_inc", "wait_op"}

    class Eng:
        def __init__(self, raw, auto=False):
            object.__setattr__(self, "_raw", raw)
            object.__setattr__(self, "_auto", auto)
            object.__setattr__(self, "last", None)
        def __getattr__(self, name):
            f = getattr(self._raw, name)
            if not callable(f):
                return f
            def call(*a, **k):
                r = f(*a, **k)
                object.__setattr__(self, "last", r)
                if self._auto and name not in _NO_DRAIN:
                    self._raw.drain()
                return r
            return call
        def inc(self, sem, n=1):
            self._raw.drain()
            self._raw.nop().then_inc(sem, n)

        incd = inc

    gP_cm = nc.semaphore("gPhase")
    gP = gP_cm.__enter__()
    phase_no = [0]

    def open_phase():
        phase_no[0] += 1
        return phase_no[0]

    def clears(e, sems_list, pn):
        for s in sems_list:
            e.sem_clear(s)
        e.sem_inc(gP, 1)

    def join(e, pn):
        e.wait_ge(gP, pn)

    def _contig_groups(sems_list):
        ss = sorted(sems_list, key=lambda s: s.num)
        groups = []
        for s in ss:
            if groups and s.num == groups[-1][-1].num + 1 and len(groups[-1]) < 4:
                groups[-1].append(s)
            else:
                groups.append([s])
        return groups

    xT = nc.dram_tensor("xT", [NB, D, S_LEN], F32, kind="ExternalInput").ap()
    wqT = nc.dram_tensor("wqT", [D, D], F32, kind="ExternalInput").ap()
    wkT = nc.dram_tensor("wkT", [D, D], F32, kind="ExternalInput").ap()
    wvT = nc.dram_tensor("wvT", [D, D], F32, kind="ExternalInput").ap()
    woT = nc.dram_tensor("woT", [D, D], F32, kind="ExternalInput").ap()
    w1 = nc.dram_tensor("w1", [NE, D, 512], F32, kind="ExternalInput").ap()
    w3 = nc.dram_tensor("w3", [NE, D, 512], F32, kind="ExternalInput").ap()
    w2 = nc.dram_tensor("w2", [NE, 512, D], F32, kind="ExternalInput").ap()
    rw = nc.dram_tensor("rw", [D, 128], F32, kind="ExternalInput").ap()
    rbias = nc.dram_tensor("rbias", [1, 8], F32, kind="ExternalInput").ap()
    tch = nc.dram_tensor("tch", [128, S_LEN], F32, kind="ExternalInput").ap()
    tcl = nc.dram_tensor("tcl", [128, S_LEN], F32, kind="ExternalInput").ap()
    tsh = nc.dram_tensor("tsh", [128, S_LEN], F32, kind="ExternalInput").ap()
    tsl = nc.dram_tensor("tsl", [128, S_LEN], F32, kind="ExternalInput").ap()
    p128T = nc.dram_tensor("p128T", [128, 128], F32, kind="ExternalInput").ap()
    ident_b = nc.dram_tensor("ident_b", [128, 128], BF16, kind="ExternalInput").ap()
    ident_f = nc.dram_tensor("ident_f", [128, 128], F32, kind="ExternalInput").ap()
    inds_d = nc.dram_tensor("inds", [8, 8 * 128], F32, kind="ExternalInput").ap()

    y = nc.dram_tensor("y", [NB, D, S_LEN], F32, kind="ExternalOutput").ap()

    x1s = nc.dram_tensor("x1s", [NB, DC, 128, S_LEN], F32, kind="Internal").ap()
    xnhs = nc.dram_tensor("xnhs", [DC, 128, S_LEN], BF16, kind="Internal").ap()
    xnls = nc.dram_tensor("xnls", [DC, 128, S_LEN], BF16, kind="Internal").ap()
    aohs = nc.dram_tensor("aohs", [NPAIR, 128, S_LEN], BF16, kind="Internal").ap()
    aols = nc.dram_tensor("aols", [NPAIR, 128, S_LEN], BF16, kind="Internal").ap()
    tap = nc.dram_tensor("tap", [128, S_LEN], F32, kind="ExternalOutput").ap()

    acc = nc.alloc_sbuf_tensor("acc", [128, DC * S_LEN], F32).ap()
    scT = nc.alloc_sbuf_tensor("scT", [8, S_LEN], F32).ap()
    rw_hi = nc.alloc_sbuf_tensor("rw_hi", [128, DC * 128], BF16).ap()
    rw_lo = nc.alloc_sbuf_tensor("rw_lo", [128, DC * 128], BF16).ap()
    rb_b = nc.alloc_sbuf_tensor("rb_b", [128, 8], F32).ap()
    ones1 = nc.alloc_sbuf_tensor("ones1", [1, 128], F32).ap()
    onescol = nc.alloc_sbuf_tensor("onescol", [128, 1], F32).ap()
    identb_s = nc.alloc_sbuf_tensor("identb_s", [128, 128], BF16).ap()
    identf_s = nc.alloc_sbuf_tensor("identf_s", [128, 128], F32).ap()
    consts = nc.alloc_sbuf_tensor("consts", [128, 8], F32).ap()
    epsrow = consts[0:1, 0:1]
    negbig = consts[:, 1:2]
    rwf = nc.alloc_sbuf_tensor("rwf", [128, DC * 128], F32).ap()
    inds_s = nc.alloc_sbuf_tensor("inds_s", [8, 8 * 128], F32).ap()
    rb_row = nc.alloc_sbuf_tensor("rb_row", [1, 8], F32).ap()

    def mm3(t, out_ap, lhi, llo, rhi, rlo, first, last):
        t.matmul(out_ap, lhi, rhi, start=first, stop=False)
        t.matmul(out_ap, lhi, rlo, start=False, stop=False)
        t.matmul(out_ap, llo, rhi, start=False, stop=last)

    with sem_t("c0") as c0, sem_t("c1") as c1, \
            sbuf_t("rwf", [128, DC * 128], F32) as rwf_t, nc.Block() as blk:
        rwf = rwf_t.ap()
        @blk.sync
        def _(e):
            e = Eng(e)
            e.dma_start(out=identb_s, in_=ident_b).then_inc(c0, 16)
            e.dma_start(out=identf_s, in_=ident_f).then_inc(c0, 16)
            e.dma_start(out=rwf.rearrange("p (c e) -> p c e", c=DC),
                        in_=rw.rearrange("(c p) e -> p c e", p=128)).then_inc(c0, 16)
            e.dma_start(out=rb_row, in_=rbias).then_inc(c0, 16)
            e.dma_start(out=inds_s, in_=inds_d).then_inc(c0, 16)

        @blk.scalar
        def _(e):
            e = Eng(e, auto=True)
            e.wait_ge(c0, 80)
            e.copy(out=rw_hi, in_=rwf)
            e.inc(c1, 1)

        @blk.vector
        def _(e):
            e = Eng(e, auto=True)
            e.memset(ones1, 1.0)
            e.memset(onescol, 1.0)
            e.memset(consts[:, 0:1], EPS)
            e.memset(consts[:, 1:2], -1e30)
            e.memset(consts[:, 2:3], -0.5)
            e.memset(consts[:, 3:4], 1.5)
            e.wait_ge(c1, 1)
            e.tensor_sub(rw_lo, rwf, rw_hi)

    with sem_t("c2") as c2, nc.Block() as blk:
        with psum_t("prb", [128, 8], F32) as prb_t:
            prb = prb_t.ap()

            @blk.tensor
            def _(e):
                e = Eng(e)
                e.matmul(prb, ones1, rb_row, start=True, stop=True)
                e.inc(c2, 1)

            @blk.vector
            def _(e):
                e = Eng(e, auto=True)
                e.wait_ge(c2, 1)
                e.tensor_copy(rb_b, prb)

    def ffn_stageA(xsrc):
        with ExitStack() as st, nc.Block() as blk:
            sem = {n: st.enter_context(sem_t(n)) for n in
                   ("sSQ sS1 sEp1 sEp2 sEp3 sBC sXF sXH sXN sRT "
                    "sG1 sG2 sG3 sG4 sGV2 sGT sGF").split()}
            sX2 = [st.enter_context(sem_t("sX")) for _ in range(2)]
            sSP2 = [st.enter_context(sem_t("sSP")) for _ in range(2)]
            ALLS = list(sem.values()) + sX2 + sSP2
            pn = open_phase()
            sSQ, sS1 = sem["sSQ"], sem["sS1"]
            sEp1, sEp2, sEp3, sBC = sem["sEp1"], sem["sEp2"], sem["sEp3"], sem["sBC"]
            sXF, sXH, sXN, sRT = sem["sXF"], sem["sXH"], sem["sXN"], sem["sRT"]
            sG1, sG2, sG3, sG4 = sem["sG1"], sem["sG2"], sem["sG3"], sem["sG4"]
            sGV2, sGT, sGF = sem["sGV2"], sem["sGT"], sem["sGF"]

            pLN1 = st.enter_context(psum_t("pLN1", [1, TT], F32)).ap()
            pLN2 = st.enter_context(psum_t("pLN2", [1, TT], F32)).ap()
            pB1 = st.enter_context(psum_t("pB1", [128, TT], F32)).ap()
            pB2 = st.enter_context(psum_t("pB2", [128, TT], F32)).ap()
            pRT = [st.enter_context(psum_t(f"pRT{i}", [128, 128], F32)).ap()
                   for i in range(2)]
            pGT = [st.enter_context(psum_t(f"pGT{i}", [8, 128], F32)).ap()
                   for i in range(2)]

            xnh = st.enter_context(sbuf_t("xnh", [128, 2 * DC * TT], BF16)).ap()
            xnl = st.enter_context(sbuf_t("xnl", [128, 2 * DC * TT], BF16)).ap()
            big = st.enter_context(sbuf_t("big", [128, DC * TT], F32)).ap()
            lnrow = st.enter_context(sbuf_t("lnrow", [1, 12 * TT], F32)).ap()

            def lnr(i):
                return lnrow[0:1, i * TT:(i + 1) * TT]
            lg = st.enter_context(sbuf_t("lg", [128, 8], F32)).ap()
            eg0 = st.enter_context(sbuf_t("eg0", [128, 8], F32)).ap()
            gsc = st.enter_context(sbuf_t("gsc", [128, 8], F32)).ap()
            grow = st.enter_context(sbuf_t("grow", [128, 32], F32)).ap()

            def accv(tt, dc):
                return acc[:, dc * S_LEN + tt * TT: dc * S_LEN + (tt + 1) * TT]

            def xnv(x, tt, dc):
                base = (tt % 2) * DC * TT
                return x[:, base + dc * TT: base + (dc + 1) * TT]

            def xslot(x, tt):
                base = (tt % 2) * DC * TT
                return x[:, base: base + DC * TT]

            @blk.sync
            def _(e):
                e = Eng(e)
                clears(e, ALLS, pn)
                if xsrc is not None:
                    for tt in range(NT):
                        if tt >= 2:
                            e.wait_ge(sSQ, tt - 1)
                            e.wait_ge(sS1, tt - 1)
                        e.dma_start(
                            out=acc.rearrange("p (c t) -> p c t", c=DC)[:, :, tt * TT:(tt + 1) * TT],
                            in_=xsrc.rearrange("(c p) t -> p c t", p=128)[:, :, tt * TT:(tt + 1) * TT],
                        ).then_inc(sX2[tt % 2], 16)

            @blk.vector
            def _(e):
                e = Eng(e, auto=True)
                join(e, pn)
                for tt in range(NT):
                    if xsrc is not None:
                        e.wait_ge(sX2[tt % 2], 16 * (tt // 2 + 1))
                    for dc in range(DC):
                        e.tensor_mul(big[:, dc * TT:(dc + 1) * TT], accv(tt, dc), accv(tt, dc))
                    e.inc(sSQ, 1)
                    e.wait_ge(sEp1, 1 + 2 * tt)
                    e.tensor_mul(lnr(2), lnr(0), lnr(0))
                    e.tensor_sub(lnr(3), lnr(1), lnr(2))
                    e.tensor_scalar_add(lnr(4), lnr(3), epsrow)
                    e.inc(sEp2, 1)
                    e.wait_ge(sEp1, 2 + 2 * tt)
                    e.reciprocal(lnr(6), lnr(5))
                    e.tensor_mul(lnr(7), lnr(6), lnr(6))
                    e.tensor_mul(lnr(8), lnr(7), lnr(4))
                    e.tensor_scalar(lnr(9), lnr(8), consts[0:1, 2:3], consts[0:1, 3:4],
                                    ALU.mult, ALU.add)
                    e.tensor_mul(lnr(10), lnr(6), lnr(9))
                    e.tensor_mul(lnr(11), lnr(0), lnr(10))
                    e.inc(sEp3, 1)
                    e.wait_ge(sBC, tt + 1)
                    if tt >= 2:
                        e.wait_ge(sSP2[tt % 2], 32 * (tt // 2))
                        e.wait_ge(sRT, 4 * (tt - 1))
                    for dc in range(DC):
                        e.tensor_mul(big[:, dc * TT:(dc + 1) * TT], accv(tt, dc), pB1)
                        e.tensor_sub(big[:, dc * TT:(dc + 1) * TT],
                                     big[:, dc * TT:(dc + 1) * TT], pB2)
                    e.inc(sXF, 1)
                    e.wait_ge(sXH, tt + 1)
                    for dc in range(DC):
                        e.tensor_sub(xnv(xnl, tt, dc), big[:, dc * TT:(dc + 1) * TT],
                                     xnv(xnh, tt, dc))
                    e.inc(sXN, 1)
                    for tc in range(4):
                        k = 4 * tt + tc
                        e.wait_ge(sRT, k + 1)
                        if k >= 1:
                            e.wait_ge(sGT, k)
                        e.tensor_add(lg, pRT[k % 2][:, 0:8], rb_b)
                        e.reduce_max(gsc[:, 0:1], lg, axis=AX.X, negate=True)
                        e.reduce_max(gsc[:, 1:2], lg, axis=AX.X)
                        e.drain()
                        e.tensor_scalar(grow[:, 0:8], lg, gsc[:, 1:2], None, ALU.subtract)
                        e.inc(sG1, 1)
                        e.wait_ge(sG2, k + 1)
                        e.scalar_tensor_tensor(grow[:, 8:16], grow[:, 0:8], negbig, lg,
                                               ALU.mult, ALU.add)
                        e.reduce_max(gsc[:, 2:3], grow[:, 8:16], axis=AX.X)
                        e.drain()
                        e.tensor_scalar(grow[:, 16:24], lg, gsc[:, 2:3], None, ALU.subtract)
                        e.inc(sG3, 1)
                        e.wait_ge(sG4, k + 1)
                        e.tensor_scalar_min(grow[:, 24:32], grow[:, 16:24], 1.0)
                        e.tensor_mul(grow[:, 0:8], eg0, grow[:, 24:32])
                        e.reduce_sum(gsc[:, 3:4], grow[:, 0:8], axis=AX.X)
                        e.drain()
                        e.reciprocal(gsc[:, 4:5], gsc[:, 3:4])
                        e.drain()
                        e.tensor_scalar(lg, grow[:, 0:8], gsc[:, 4:5], None, ALU.mult)
                        e.inc(sGV2, 1)

            @blk.scalar
            def _(e):
                e = Eng(e, auto=True)
                join(e, pn)
                for tt in range(NT):
                    e.wait_ge(sS1, tt + 1)
                    e.activation(lnr(0), pLN1, AF.Copy, scale=1.0 / D)
                    e.activation(lnr(1), pLN2, AF.Copy, scale=1.0 / D)
                    e.inc(sEp1, 1)
                    e.wait_ge(sEp2, tt + 1)
                    e.activation(lnr(5), lnr(4), AF.Sqrt)
                    e.inc(sEp1, 1)
                    e.wait_ge(sXF, tt + 1)
                    if tt >= 2:
                        e.wait_ge(sRT, 4 * (tt - 1))
                    for dc in range(DC):
                        e.copy(out=xnv(xnh, tt, dc), in_=big[:, dc * TT:(dc + 1) * TT])
                    e.inc(sXH, 1)
                    e.wait_ge(sXN, tt + 1)
                    e.dma_start(
                        out=xnhs.rearrange("c p t -> p c t")[:, :, tt * TT:(tt + 1) * TT],
                        in_=xslot(xnh, tt).rearrange("p (c t) -> p c t", c=DC),
                    ).then_inc(sSP2[tt % 2], 16)
                    e.dma_start(
                        out=xnls.rearrange("c p t -> p c t")[:, :, tt * TT:(tt + 1) * TT],
                        in_=xslot(xnl, tt).rearrange("p (c t) -> p c t", c=DC),
                    ).then_inc(sSP2[tt % 2], 16)
                    for tc in range(4):
                        k = 4 * tt + tc
                        e.wait_ge(sG1, k + 1)
                        e.activation(grow[:, 8:16], grow[:, 0:8], AF.Sign)
                        e.drain()
                        e.add(out=grow[:, 0:8], in_=grow[:, 8:16], add=1.0)
                        e.activation(eg0, lg, AF.Exp, bias=gsc[:, 0:1])
                        e.inc(sG2, 1)
                        e.wait_ge(sG3, k + 1)
                        e.activation(grow[:, 24:32], grow[:, 16:24], AF.Sign)
                        e.drain()
                        e.add(out=grow[:, 16:24], in_=grow[:, 24:32], add=1.0)
                        e.inc(sG4, 1)
                        e.wait_ge(sGT, k + 1)
                        import os as _os
                        if _os.environ.get("KSCT"):
                            e.memzero(scT[0:8, k * 128:(k + 1) * 128])
                        else:
                            e.activation(out=scT[0:8, k * 128:(k + 1) * 128],
                                         in_=pGT[k % 2], func=AF.Copy, scale=0.5)
                        e.inc(sGF, 1)
                import os as _os2
                if _os2.environ.get("KTAP"):
                    e.wait_ge(sGF, 16)
                    e.dma_start(out=tap[0:8, :], in_=scT).then_inc(sGF, 16)
                    e.dma_start(out=tap[:, 256:264], in_=lg).then_inc(sGF, 16)
                    e.dma_start(out=tap[:, 288:320], in_=grow).then_inc(sGF, 16)
                    e.dma_start(out=tap[:, 320:328], in_=gsc).then_inc(sGF, 16)
                    e.dma_start(out=tap[:, 328:336], in_=eg0).then_inc(sGF, 16)
                    e.dma_start(out=tap[:, 336:344], in_=rb_b).then_inc(sGF, 16)
                    e.wait_ge(sGF, 112)

            @blk.tensor
            def _(e):
                e = Eng(e)
                join(e, pn)
                for tt in range(NT):
                    if xsrc is not None:
                        e.wait_ge(sX2[tt % 2], 16 * (tt // 2 + 1))
                    e.wait_ge(sSQ, tt + 1)
                    if tt >= 1:
                        e.wait_ge(sEp1, 2 * tt - 1)
                    for dc in range(DC):
                        e.matmul(pLN1, onescol, accv(tt, dc),
                                 start=(dc == 0), stop=(dc == DC - 1))
                    for dc in range(DC):
                        e.matmul(pLN2, onescol, big[:, dc * TT:(dc + 1) * TT],
                                 start=(dc == 0), stop=(dc == DC - 1))
                    e.inc(sS1, 1)
                    e.wait_ge(sEp3, tt + 1)
                    if tt >= 1:
                        e.wait_ge(sXF, tt)
                    e.matmul(pB1, ones1, lnr(10), start=True, stop=True)
                    e.matmul(pB2, ones1, lnr(11), start=True, stop=True)
                    e.inc(sBC, 1)
                    e.wait_ge(sXN, tt + 1)
                    for tc in range(4):
                        k = 4 * tt + tc
                        if k >= 2:
                            e.wait_ge(sG1, k - 1)
                        xh = xslot(xnh, tt)
                        xl = xslot(xnl, tt)
                        for dc in range(DC):
                            c0_ = dc * TT + tc * 128
                            mm3(e, pRT[k % 2],
                                xh[:, c0_:c0_ + 128], xl[:, c0_:c0_ + 128],
                                rw_hi[:, dc * 128:(dc + 1) * 128],
                                rw_lo[:, dc * 128:(dc + 1) * 128],
                                dc == 0, dc == DC - 1)
                        e.inc(sRT, 1)
                        e.wait_ge(sGV2, k + 1)
                        if k >= 2:
                            e.wait_ge(sGF, k - 1)
                        e.matmul(pGT[k % 2], lg, identf_s, start=True, stop=True)
                        e.inc(sGT, 1)
                e.wait_ge(sSP2[0], 64)
                e.wait_ge(sSP2[1], 64)
                e.wait_ge(sGF, 16)

    def ffn_stageB():
        import os
        NE_RUN = int(os.environ.get("KNE", NE))
        KNOACC = os.environ.get("KNOACC")
        KNOSILU = os.environ.get("KNOSILU")
        KNOH = os.environ.get("KNOH")
        with ExitStack() as st, nc.Block() as blk:
            sem = {n: st.enter_context(sem_t(n)) for n in
                   "sWF sWL sGB sGBE sH sSIL sHM sHH sHL sY sYE".split()}
            sWHc = st.enter_context(sem_t("sWHc"))
            sXT2 = [st.enter_context(sem_t("sXT")) for _ in range(2)]
            ALLS = list(sem.values()) + [sWHc] + sXT2
            pn = open_phase()
            sWF, sWL = sem["sWF"], sem["sWL"]
            sGB, sGBE, sH, sSIL = sem["sGB"], sem["sGBE"], sem["sH"], sem["sSIL"]
            sHM, sHH, sHL = sem["sHM"], sem["sHH"], sem["sHL"]
            sY, sYE = sem["sY"], sem["sYE"]

            ph = [st.enter_context(psum_t(f"ph{i}", [128, TT], F32)).ap()
                  for i in range(4)]
            pY = [st.enter_context(psum_t(f"pY{i}", [128, TT], F32)).ap()
                  for i in range(2)]
            pGB = [st.enter_context(psum_t(f"pGB{i}", [128, TT], F32)).ap()
                   for i in range(2)]

            wr_hi = st.enter_context(sbuf_t("wr_hi", [128, 3 * DC * TT], BF16)).ap()
            wr_lo = st.enter_context(sbuf_t("wr_lo", [128, 3 * DC * TT], BF16)).ap()
            wtmp = st.enter_context(sbuf_t("wtmp", [128, DC * TT], F32)).ap()
            xth = st.enter_context(sbuf_t("xth", [128, 2 * DC * TT], BF16)).ap()
            xtl = st.enter_context(sbuf_t("xtl", [128, 2 * DC * TT], BF16)).ap()
            hhi = st.enter_context(sbuf_t("hhi", [128, HH * TT], BF16)).ap()
            hlo = st.enter_context(sbuf_t("hlo", [128, HH * TT], BF16)).ap()
            sil = st.enter_context(sbuf_t("sil", [128, 2 * TT], F32)).ap()
            hgt = st.enter_context(sbuf_t("hgt", [128, 2 * TT], F32)).ap()
            gbs = st.enter_context(sbuf_t("gbs", [128, 2 * TT], F32)).ap()

            NM = 3 * NE_RUN

            def wslot(x, m):
                return x[:, (m % 3) * DC * TT:(m % 3 + 1) * DC * TT]

            def xs(x, k):
                return x[:, (k % 2) * DC * TT:(k % 2 + 1) * DC * TT]

            def p2(x, j):
                return x[:, (j % 2) * TT:(j % 2 + 1) * TT]

            def w_src(m):
                e_, j = divmod(m, 3)
                t = [w1, w3, w2][j]
                c = DC if j < 2 else HH
                return t[e_].rearrange("(c p) h -> p c h", p=128), c

            @blk.sync
            def _(e):
                e = Eng(e)
                clears(e, ALLS, pn)
                for m in range(3):
                    if m >= 1:
                        e.wait_ge(sWL, m)
                    src3, c = w_src(m)
                    e.dma_start(out=wtmp.rearrange("p (c h) -> p c h", c=c),
                                in_=src3).then_inc(sWF, 16)
                for ee in range(NE_RUN):
                    for tt in range(NT):
                        k = ee * NT + tt
                        if k >= 2:
                            e.wait_ge(sH, 2 * k - 2)
                        e.dma_start(out=xs(xth, k).rearrange("p (c t) -> p c t", c=DC),
                                    in_=xnhs.rearrange("c p t -> p c t")[:, :, tt * TT:(tt + 1) * TT],
                                    ).then_inc(sXT2[k % 2], 16)
                        e.dma_start(out=xs(xtl, k).rearrange("p (c t) -> p c t", c=DC),
                                    in_=xnls.rearrange("c p t -> p c t")[:, :, tt * TT:(tt + 1) * TT],
                                    ).then_inc(sXT2[k % 2], 16)
                    if ee + 1 < NE_RUN:
                        for j in range(3):
                            m = 3 * (ee + 1) + j
                            e.wait_ge(sWL, m)
                            src3, c = w_src(m)
                            e.dma_start(out=wtmp.rearrange("p (c h) -> p c h", c=c),
                                        in_=src3).then_inc(sWF, 16)

            @blk.vector
            def _(e):
                e = Eng(e, auto=True)
                join(e, pn)
                for ee in range(NE_RUN):
                    for j in range(3):
                        m = 3 * ee + j
                        e.wait_ge(sWHc, m + 1)
                        e.tensor_sub(wslot(wr_lo, m), wtmp, wslot(wr_hi, m))
                        e.inc(sWL, 1)
                    for tt in range(NT):
                        k = ee * NT + tt
                        e.wait_ge(sGB, k + 1)
                        if ee < 8 and not os.environ.get("KGB1"):
                            e.tensor_copy(p2(gbs, k), pGB[k % 2])
                        else:
                            e.memset(p2(gbs, k), 0.5)
                        e.inc(sGBE, 1)
                        for half in range(2):
                            kk = 2 * k + half
                            e.wait_ge(sSIL, 2 * kk + 2)
                            for c in range(2):
                                if KNOH:
                                    e.memset(p2(hgt, c), 0.01)
                                elif os.environ.get("KNOGB"):
                                    e.tensor_mul(p2(hgt, c), p2(sil, c), ph[2 + c])
                                elif os.environ.get("KNOPS"):
                                    e.tensor_mul(p2(hgt, c), p2(sil, c), p2(gbs, k))
                                else:
                                    e.tensor_mul(p2(hgt, c), p2(sil, c), ph[2 + c])
                                    e.tensor_mul(p2(hgt, c), p2(hgt, c), p2(gbs, k))
                            e.inc(sHM, 1)
                            e.wait_ge(sHH, kk + 1)
                            for c in range(2):
                                hc = 2 * half + c
                                e.tensor_sub(hlo[:, hc * TT:(hc + 1) * TT],
                                             p2(hgt, c), hhi[:, hc * TT:(hc + 1) * TT])
                            e.inc(sHL, 1)
                        for dc in range(DC):
                            kd = k * DC + dc
                            e.wait_ge(sY, kd + 1)
                            col = dc * S_LEN + tt * TT
                            if KNOACC:
                                e.tensor_copy(big[0:1, 0:TT] if False else hgt[:, 0:TT], pY[dc % 2])
                            else:
                                e.tensor_add(acc[:, col:col + TT], acc[:, col:col + TT],
                                             pY[dc % 2])
                            e.inc(sYE, 1)

            @blk.scalar
            def _(e):
                e = Eng(e, auto=True)
                join(e, pn)
                for ee in range(NE_RUN):
                    for j in range(3):
                        m = 3 * ee + j
                        e.wait_ge(sWF, 16 * (m + 1))
                        if m >= 3:
                            if j == 2:
                                e.wait_ge(sY, DC * NT * ee)
                            else:
                                e.wait_ge(sH, 2 * NT * ee)
                        e.copy(out=wslot(wr_hi, m), in_=wtmp)
                        e.inc(sWHc, 1)
                    for tt in range(NT):
                        k = ee * NT + tt
                        for half in range(2):
                            kk = 2 * k + half
                            e.wait_ge(sH, kk + 1)
                            for c in range(2):
                                if KNOSILU:
                                    e.copy(out=p2(sil, c), in_=ph[c])
                                else:
                                    e.activation(p2(sil, c), ph[c], AF.Silu)
                                e.inc(sSIL, 1)
                            e.wait_ge(sHM, kk + 1)
                            for c in range(2):
                                hc = 2 * half + c
                                e.copy(out=hhi[:, hc * TT:(hc + 1) * TT], in_=p2(hgt, c))
                            e.inc(sHH, 1)

            @blk.tensor
            def _(e):
                e = Eng(e)
                join(e, pn)
                for ee in range(NE_RUN):
                    for tt in range(NT):
                        k = ee * NT + tt
                        e.wait_ge(sXT2[k % 2], 32 * (k // 2 + 1))
                        xh = xs(xth, k)
                        xl = xs(xtl, k)
                        if ee < 8:
                            if k >= 2:
                                e.wait_ge(sGBE, k - 1)
                            e.matmul(pGB[k % 2], inds_s[:, ee * 128:(ee + 1) * 128],
                                     scT[:, tt * TT:(tt + 1) * TT], start=True, stop=True)
                            e.inc(sGB, 1)
                        else:
                            e.nop().then_inc(sGB, 1)
                        e.wait_ge(sWL, 3 * ee + 2)
                        for half in range(2):
                            kk = 2 * k + half
                            if kk >= 1:
                                e.wait_ge(sHM, kk)
                            w1h, w1l = wslot(wr_hi, 3 * ee), wslot(wr_lo, 3 * ee)
                            w3h, w3l = wslot(wr_hi, 3 * ee + 1), wslot(wr_lo, 3 * ee + 1)
                            for c in range(2):
                                hc = 2 * half + c
                                for dc in range(DC):
                                    co = dc * TT + hc * 128
                                    mm3(e, ph[c],
                                        w1h[:, co:co + 128], w1l[:, co:co + 128],
                                        xh[:, dc * TT:(dc + 1) * TT],
                                        xl[:, dc * TT:(dc + 1) * TT],
                                        dc == 0, dc == DC - 1)
                            for c in range(2):
                                hc = 2 * half + c
                                for dc in range(DC):
                                    co = dc * TT + hc * 128
                                    mm3(e, ph[2 + c],
                                        w3h[:, co:co + 128], w3l[:, co:co + 128],
                                        xh[:, dc * TT:(dc + 1) * TT],
                                        xl[:, dc * TT:(dc + 1) * TT],
                                        dc == 0, dc == DC - 1)
                            e.inc(sH, 1)
                        e.wait_ge(sWL, 3 * ee + 3)
                        e.wait_ge(sHL, 2 * k + 2)
                        e.wait_ge(sGBE, k + 1)
                        w2h, w2l = wslot(wr_hi, 3 * ee + 2), wslot(wr_lo, 3 * ee + 2)
                        for dc in range(DC):
                            kd = k * DC + dc
                            if kd >= 2:
                                e.wait_ge(sYE, kd - 1)
                            for hc in range(HH):
                                co = hc * D + dc * 128
                                mm3(e, pY[dc % 2],
                                    w2h[:, co:co + 128], w2l[:, co:co + 128],
                                    hhi[:, hc * TT:(hc + 1) * TT],
                                    hlo[:, hc * TT:(hc + 1) * TT],
                                    hc == 0, hc == HH - 1)
                            e.inc(sY, 1)
                e.wait_ge(sYE, NE_RUN * NT * DC)

    def spill_acc(dst):
        with sem_t("sSPA") as s, nc.Block() as blk:
            ALLS = [s]
            pn = open_phase()
            @blk.sync
            def _(e):
                e = Eng(e)
                clears(e, ALLS, pn)
                e.dma_start(out=dst, in_=acc.rearrange("p (c t) -> p c t", c=DC)).then_inc(s, 16)
                e.wait_ge(s, 16)

    def attn_ln(b):
        with ExitStack() as st, nc.Block() as blk:
            sem = {n: st.enter_context(sem_t(n)) for n in
                   "sSQ sS1 sEp1 sEp2 sEp3 sBC sXF sXH sXN".split()}
            sX2 = [st.enter_context(sem_t("sX")) for _ in range(2)]
            sSP2 = [st.enter_context(sem_t("sSP")) for _ in range(2)]
            ALLS = list(sem.values()) + sX2 + sSP2
            pn = open_phase()
            sSQ, sS1 = sem["sSQ"], sem["sS1"]
            sEp1, sEp2, sEp3, sBC = sem["sEp1"], sem["sEp2"], sem["sEp3"], sem["sBC"]
            sXF, sXH, sXN = sem["sXF"], sem["sXH"], sem["sXN"]

            pLN1 = st.enter_context(psum_t("pLN1", [1, TT], F32)).ap()
            pLN2 = st.enter_context(psum_t("pLN2", [1, TT], F32)).ap()
            pB1 = st.enter_context(psum_t("pB1", [128, TT], F32)).ap()
            pB2 = st.enter_context(psum_t("pB2", [128, TT], F32)).ap()

            x1t = st.enter_context(sbuf_t("x1t", [128, 2 * DC * TT], F32)).ap()
            xnh = st.enter_context(sbuf_t("xnh", [128, 2 * DC * TT], BF16)).ap()
            xnl = st.enter_context(sbuf_t("xnl", [128, 2 * DC * TT], BF16)).ap()
            big = st.enter_context(sbuf_t("big", [128, DC * TT], F32)).ap()
            lnrow = st.enter_context(sbuf_t("lnrow", [1, 12 * TT], F32)).ap()

            def lnr(i):
                return lnrow[0:1, i * TT:(i + 1) * TT]

            def x1v(tt, dc):
                base = (tt % 2) * DC * TT
                return x1t[:, base + dc * TT: base + (dc + 1) * TT]

            def xnv(x, tt, dc):
                base = (tt % 2) * DC * TT
                return x[:, base + dc * TT: base + (dc + 1) * TT]

            def xslot(x, tt):
                return x[:, (tt % 2) * DC * TT:(tt % 2 + 1) * DC * TT]

            @blk.sync
            def _(e):
                e = Eng(e)
                clears(e, ALLS, pn)
                for tt in range(NT):
                    if tt >= 2:
                        e.wait_ge(sXN, tt - 1)
                    e.dma_start(
                        out=xslot(x1t, tt).rearrange("p (c t) -> p c t", c=DC),
                        in_=x1s[b][:, :, tt * TT:(tt + 1) * TT].rearrange("c p t -> p c t"),
                    ).then_inc(sX2[tt % 2], 16)

            @blk.vector
            def _(e):
                e = Eng(e, auto=True)
                join(e, pn)
                for tt in range(NT):
                    e.wait_ge(sX2[tt % 2], 16 * (tt // 2 + 1))
                    for dc in range(DC):
                        e.tensor_mul(big[:, dc * TT:(dc + 1) * TT], x1v(tt, dc), x1v(tt, dc))
                    e.inc(sSQ, 1)
                    e.wait_ge(sEp1, 1 + 2 * tt)
                    e.tensor_mul(lnr(2), lnr(0), lnr(0))
                    e.tensor_sub(lnr(3), lnr(1), lnr(2))
                    e.tensor_scalar_add(lnr(4), lnr(3), epsrow)
                    e.inc(sEp2, 1)
                    e.wait_ge(sEp1, 2 + 2 * tt)
                    e.reciprocal(lnr(6), lnr(5))
                    e.tensor_mul(lnr(7), lnr(6), lnr(6))
                    e.tensor_mul(lnr(8), lnr(7), lnr(4))
                    e.tensor_scalar(lnr(9), lnr(8), consts[0:1, 2:3], consts[0:1, 3:4],
                                    ALU.mult, ALU.add)
                    e.tensor_mul(lnr(10), lnr(6), lnr(9))
                    e.tensor_mul(lnr(11), lnr(0), lnr(10))
                    e.inc(sEp3, 1)
                    e.wait_ge(sBC, tt + 1)
                    if tt >= 2:
                        e.wait_ge(sSP2[tt % 2], 32 * (tt // 2))
                    for dc in range(DC):
                        e.tensor_mul(big[:, dc * TT:(dc + 1) * TT], x1v(tt, dc), pB1)
                        e.tensor_sub(big[:, dc * TT:(dc + 1) * TT],
                                     big[:, dc * TT:(dc + 1) * TT], pB2)
                    e.inc(sXF, 1)
                    e.wait_ge(sXH, tt + 1)
                    for dc in range(DC):
                        e.tensor_sub(xnv(xnl, tt, dc), big[:, dc * TT:(dc + 1) * TT],
                                     xnv(xnh, tt, dc))
                    e.inc(sXN, 1)

            @blk.scalar
            def _(e):
                e = Eng(e, auto=True)
                join(e, pn)
                for tt in range(NT):
                    e.wait_ge(sS1, tt + 1)
                    e.activation(lnr(0), pLN1, AF.Copy, scale=1.0 / D)
                    e.activation(lnr(1), pLN2, AF.Copy, scale=1.0 / D)
                    e.inc(sEp1, 1)
                    e.wait_ge(sEp2, tt + 1)
                    e.activation(lnr(5), lnr(4), AF.Sqrt)
                    e.inc(sEp1, 1)
                    e.wait_ge(sXF, tt + 1)
                    for dc in range(DC):
                        e.copy(out=xnv(xnh, tt, dc), in_=big[:, dc * TT:(dc + 1) * TT])
                    e.inc(sXH, 1)
                    e.wait_ge(sXN, tt + 1)
                    e.dma_start(
                        out=xnhs.rearrange("c p t -> p c t")[:, :, tt * TT:(tt + 1) * TT],
                        in_=xslot(xnh, tt).rearrange("p (c t) -> p c t", c=DC),
                    ).then_inc(sSP2[tt % 2], 16)
                    e.dma_start(
                        out=xnls.rearrange("c p t -> p c t")[:, :, tt * TT:(tt + 1) * TT],
                        in_=xslot(xnl, tt).rearrange("p (c t) -> p c t", c=DC),
                    ).then_inc(sSP2[tt % 2], 16)

            @blk.tensor
            def _(e):
                e = Eng(e)
                join(e, pn)
                for tt in range(NT):
                    e.wait_ge(sX2[tt % 2], 16 * (tt // 2 + 1))
                    e.wait_ge(sSQ, tt + 1)
                    if tt >= 1:
                        e.wait_ge(sEp1, 2 * tt - 1)
                    for dc in range(DC):
                        e.matmul(pLN1, onescol, x1v(tt, dc),
                                 start=(dc == 0), stop=(dc == DC - 1))
                    for dc in range(DC):
                        e.matmul(pLN2, onescol, big[:, dc * TT:(dc + 1) * TT],
                                 start=(dc == 0), stop=(dc == DC - 1))
                    e.inc(sS1, 1)
                    e.wait_ge(sEp3, tt + 1)
                    if tt >= 1:
                        e.wait_ge(sXF, tt)
                    e.matmul(pB1, ones1, lnr(10), start=True, stop=True)
                    e.matmul(pB2, ones1, lnr(11), start=True, stop=True)
                    e.inc(sBC, 1)
                e.wait_ge(sSP2[0], 64)
                e.wait_ge(sSP2[1], 64)

    def attn_pair(pair):
        with ExitStack() as st:
            wqh = st.enter_context(sbuf_t("wqh", [128, DC * 128], BF16)).ap()
            wql = st.enter_context(sbuf_t("wql", [128, DC * 128], BF16)).ap()
            wkh = st.enter_context(sbuf_t("wkh", [128, DC * 128], BF16)).ap()
            wkl = st.enter_context(sbuf_t("wkl", [128, DC * 128], BF16)).ap()
            wvh = st.enter_context(sbuf_t("wvh", [128, DC * 128], BF16)).ap()
            wvl = st.enter_context(sbuf_t("wvl", [128, DC * 128], BF16)).ap()
            khi = st.enter_context(sbuf_t("khi", [128, S_LEN], BF16)).ap()
            klo = st.enter_context(sbuf_t("klo", [128, S_LEN], BF16)).ap()
            qhi = st.enter_context(sbuf_t("qhi", [128, S_LEN], BF16)).ap()
            qlo = st.enter_context(sbuf_t("qlo", [128, S_LEN], BF16)).ap()
            vhi = st.enter_context(sbuf_t("vhi", [128, 16 * 144], BF16)).ap()
            vlo = st.enter_context(sbuf_t("vlo", [128, 16 * 144], BF16)).ap()
            tabs = st.enter_context(sbuf_t("tabs", [128, 4 * S_LEN], F32)).ap()
            p128_s = st.enter_context(sbuf_t("p128_s", [128, 128], F32)).ap()

            def tab(i, tt):
                return tabs[:, i * S_LEN + tt * TT: i * S_LEN + (tt + 1) * TT]

            with ExitStack() as st2, nc.Block() as blk:
                sem = {n: st2.enter_context(sem_t(n)) for n in
                       "sWS sWHc sWLc sKQ sKC sPP sRO sRH sRL sVV sVH sVL".split()}
                sXT2 = [st2.enter_context(sem_t("sXT")) for _ in range(2)]
                ALLS = list(sem.values()) + sXT2
                pn = open_phase()
                sWS, sWHc, sWLc = sem["sWS"], sem["sWHc"], sem["sWLc"]
                sKQ, sKC, sPP, sRO = sem["sKQ"], sem["sKC"], sem["sPP"], sem["sRO"]
                sRH, sRL, sVV, sVH = sem["sRH"], sem["sRL"], sem["sVV"], sem["sVH"]
                sVL = sem["sVL"]

                pKQ = [st2.enter_context(psum_t(f"pKQ{i}", [128, TT], F32)).ap()
                       for i in range(2)]
                pPP = [st2.enter_context(psum_t(f"pPP{i}", [128, TT], F32)).ap()
                       for i in range(2)]
                pV = [st2.enter_context(psum_t(f"pV{i}", [128, 128], F32)).ap()
                      for i in range(2)]

                wsl = st2.enter_context(sbuf_t("wsl", [128, DC * 128], F32)).ap()
                xth = st2.enter_context(sbuf_t("xth", [128, 2 * DC * TT], BF16)).ap()
                xtl = st2.enter_context(sbuf_t("xtl", [128, 2 * DC * TT], BF16)).ap()
                kqb = st2.enter_context(sbuf_t("kqb", [128, TT], F32)).ap()
                rtm = st2.enter_context(sbuf_t("rtm", [128, 4 * TT], F32)).ap()

                whl = [(wkh, wkl), (wvh, wvl), (wqh, wql)]
                wsrc = [wkT, wvT, wqT]

                def xs(x, k):
                    return x[:, (k % 2) * DC * TT:(k % 2 + 1) * DC * TT]

                def p2(x, j):
                    return x[:, (j % 2) * TT:(j % 2 + 1) * TT]

                @blk.sync
                def _(e):
                    e = Eng(e)
                    clears(e, ALLS, pn)
                    e.dma_start(out=p128_s, in_=p128T).then_inc(sWS, 16)
                    for i, t in enumerate((tch, tcl, tsh, tsl)):
                        e.dma_start(out=tabs[:, i * S_LEN:(i + 1) * S_LEN], in_=t,
                                    ).then_inc(sWS, 16)
                    c0_ = pair * 128
                    for i in range(3):
                        if i >= 1:
                            e.wait_ge(sWLc, i)
                        e.dma_start(out=wsl.rearrange("p (c o) -> p c o", c=DC),
                                    in_=wsrc[i].rearrange("(c p) o -> p c o", p=128)[:, :, c0_:c0_ + 128],
                                    ).then_inc(sWS, 16)
                    for rnd in range(2):
                        for tt in range(NT):
                            k = rnd * NT + tt
                            if k >= 2:
                                prev = k - 2
                                if prev < 4:
                                    e.wait_ge(sVV, 4 * prev + 4)
                                else:
                                    e.wait_ge(sKQ, prev + 1)
                            e.dma_start(out=xs(xth, k).rearrange("p (c t) -> p c t", c=DC),
                                        in_=xnhs.rearrange("c p t -> p c t")[:, :, tt * TT:(tt + 1) * TT],
                                        ).then_inc(sXT2[k % 2], 16)
                            e.dma_start(out=xs(xtl, k).rearrange("p (c t) -> p c t", c=DC),
                                        in_=xnls.rearrange("c p t -> p c t")[:, :, tt * TT:(tt + 1) * TT],
                                        ).then_inc(sXT2[k % 2], 16)

                base = 80

                @blk.scalar
                def _(e):
                    e = Eng(e, auto=True)
                    join(e, pn)
                    for i in range(3):
                        e.wait_ge(sWS, base + 16 * (i + 1))
                        e.copy(out=whl[i][0], in_=wsl)
                        e.inc(sWHc, 1)
                    for et in range(8):
                        tt = et % NT
                        e.wait_ge(sRO, et + 1)
                        dsth = khi if et < 4 else qhi
                        e.copy(out=dsth[:, tt * TT:(tt + 1) * TT], in_=rtm[:, 0:TT])
                        e.inc(sRH, 1)
                        if et < 4:
                            for tc in range(4):
                                vt = 4 * tt + tc
                                e.wait_ge(sVV, vt + 1)
                                uc = vt
                                e.copy(out=vhi[:, uc * 144:uc * 144 + 64],
                                       in_=pV[vt % 2][:, 0:64])
                                e.copy(out=vhi[:, uc * 144 + 72:uc * 144 + 136],
                                       in_=pV[vt % 2][:, 64:128])
                                e.inc(sVH, 1)

                @blk.vector
                def _(e):
                    e = Eng(e, auto=True)
                    join(e, pn)
                    e.memset(vlo, 0.0)
                    for uc in range(16):
                        e.memset(vhi[:, uc * 144 + 64:uc * 144 + 65], 1.0)
                        e.memset(vhi[:, uc * 144 + 136:uc * 144 + 137], 1.0)
                    for i in range(3):
                        e.wait_ge(sWHc, i + 1)
                        e.tensor_sub(whl[i][1], wsl, whl[i][0])
                        e.inc(sWLc, 1)
                    for et in range(8):
                        tt = et % NT
                        e.wait_ge(sKQ, et + 1)
                        if et >= 1:
                            e.wait_ge(sPP, et)
                        e.tensor_copy(kqb, pKQ[et % 2])
                        e.inc(sKC, 1)
                        e.wait_ge(sPP, et + 1)
                        e.tensor_mul(rtm[:, 0:TT], kqb, tab(0, tt))
                        e.tensor_mul(rtm[:, TT:2 * TT], kqb, tab(1, tt))
                        e.tensor_add(rtm[:, 2 * TT:3 * TT], rtm[:, 0:TT], rtm[:, TT:2 * TT])
                        e.tensor_mul(rtm[:, 0:TT], pPP[et % 2], tab(2, tt))
                        e.tensor_mul(rtm[:, TT:2 * TT], pPP[et % 2], tab(3, tt))
                        e.tensor_add(rtm[:, 3 * TT:4 * TT], rtm[:, 0:TT], rtm[:, TT:2 * TT])
                        e.tensor_add(rtm[:, 0:TT], rtm[:, 2 * TT:3 * TT], rtm[:, 3 * TT:4 * TT])
                        e.inc(sRO, 1)
                        e.wait_ge(sRH, et + 1)
                        dsth = khi if et < 4 else qhi
                        dstl = klo if et < 4 else qlo
                        e.tensor_sub(dstl[:, tt * TT:(tt + 1) * TT], rtm[:, 0:TT],
                                     dsth[:, tt * TT:(tt + 1) * TT])
                        e.inc(sRL, 1)
                        if et < 4:
                            for tc in range(4):
                                vt = 4 * tt + tc
                                uc = vt
                                e.wait_ge(sVH, vt + 1)
                                e.tensor_sub(vlo[:, uc * 144:uc * 144 + 64],
                                             pV[vt % 2][:, 0:64],
                                             vhi[:, uc * 144:uc * 144 + 64])
                                e.tensor_sub(vlo[:, uc * 144 + 72:uc * 144 + 136],
                                             pV[vt % 2][:, 64:128],
                                             vhi[:, uc * 144 + 72:uc * 144 + 136])
                                e.inc(sVL, 1)

                @blk.tensor
                def _(e):
                    e = Eng(e)
                    join(e, pn)
                    for et in range(8):
                        rnd, tt = divmod(et, NT)
                        widx = 0 if rnd == 0 else 2
                        wh, wl = whl[widx]
                        e.wait_ge(sXT2[et % 2], 32 * (et // 2 + 1))
                        e.wait_ge(sWLc, widx + 1)
                        xh, xl = xs(xth, et), xs(xtl, et)
                        if et >= 2:
                            e.wait_ge(sKC, et - 1)
                        for dc in range(DC):
                            mm3(e, pKQ[et % 2],
                                wh[:, dc * 128:(dc + 1) * 128], wl[:, dc * 128:(dc + 1) * 128],
                                xh[:, dc * TT:(dc + 1) * TT], xl[:, dc * TT:(dc + 1) * TT],
                                dc == 0, dc == DC - 1)
                        e.inc(sKQ, 1)
                        e.wait_ge(sKC, et + 1)
                        if et >= 2:
                            e.wait_ge(sRO, et - 1)
                        e.matmul(pPP[et % 2], p128_s, kqb, start=True, stop=True)
                        e.inc(sPP, 1)
                        if rnd == 0:
                            e.wait_ge(sWLc, 2)
                            for tc in range(4):
                                vt = 4 * tt + tc
                                if vt >= 2:
                                    e.wait_ge(sVL, vt - 1)
                                for dc in range(DC):
                                    mm3(e, pV[vt % 2],
                                        xh[:, dc * TT + tc * 128: dc * TT + tc * 128 + 128],
                                        xl[:, dc * TT + tc * 128: dc * TT + tc * 128 + 128],
                                        wvh[:, dc * 128:(dc + 1) * 128],
                                        wvl[:, dc * 128:(dc + 1) * 128],
                                        dc == 0, dc == DC - 1)
                                e.inc(sVV, 1)
                    e.wait_ge(sRL, 8)
                    e.wait_ge(sVL, 16)

            with ExitStack() as st2, nc.Block() as blk:
                sem = {n: st2.enter_context(sem_t(n)) for n in
                       "sSC sMX sEX sE2 sEH sEL sTR sTC sAV sAO sAH sAL sSPL".split()}
                ALLS = list(sem.values())
                pn = open_phase()
                sSC, sMX, sEX, sEH = sem["sSC"], sem["sMX"], sem["sEX"], sem["sEH"]
                sE2 = sem["sE2"]
                sEL, sTR, sTC, sAV = sem["sEL"], sem["sTR"], sem["sTC"], sem["sAV"]
                sAO, sAH = sem["sAO"], sem["sAH"]
                sAL, sSPL = sem["sAL"], sem["sSPL"]

                pSC = [st2.enter_context(psum_t(f"pSC{i}", [128, TT], F32)).ap()
                       for i in range(4)]
                pTR = [st2.enter_context(psum_t(f"pTR{i}", [128, 128], BF16)).ap()
                       for i in range(2)]
                pAV = [st2.enter_context(psum_t(f"pAV{i}", [65, 128], F32)).ap()
                       for i in range(2)]

                aoh = st2.enter_context(sbuf_t("aoh", [128, S_LEN], BF16)).ap()
                aol = st2.enter_context(sbuf_t("aol", [128, S_LEN], BF16)).ap()
                ebuf = st2.enter_context(sbuf_t("ebuf", [128, S_LEN], F32)).ap()
                ehi = st2.enter_context(sbuf_t("ehi", [128, S_LEN], BF16)).ap()
                elo = st2.enter_context(sbuf_t("elo", [128, S_LEN], BF16)).ap()
                pth = st2.enter_context(sbuf_t("pth", [128, 2 * S_LEN], BF16)).ap()
                ptl = st2.enter_context(sbuf_t("ptl", [128, 2 * S_LEN], BF16)).ap()
                mneg = st2.enter_context(sbuf_t("mneg", [128, 8], F32)).ap()
                ebuf2 = st2.enter_context(sbuf_t("ebuf2", [128, S_LEN], F32)).ap()
                aot = st2.enter_context(sbuf_t("aot", [64, 256], F32)).ap()
                aoh0 = st2.enter_context(sbuf_t("aoh0", [64, 256], BF16)).ap()
                aol0 = st2.enter_context(sbuf_t("aol0", [64, 256], BF16)).ap()

                NGI = 32
                NGG = 16

                def pt_s(x, sb2, uc):
                    return x[:, sb2 * S_LEN + uc * 128: sb2 * S_LEN + (uc + 1) * 128]

                def emit_tr(e, gp):
                    for j in range(32):
                        src = ehi if j < 16 else elo
                        uc = j % 16
                        e.wait_ge(sEH if j < 16 else sEL, gp + 1)
                        B = 32 * gp + j
                        if B >= 2:
                            e.wait_ge(sTC, B - 1)
                        e.transpose(pTR[B % 2], src[:, uc * 128:(uc + 1) * 128],
                                    identb_s)
                        e.inc(sTR, 1)

                def emit_av(e, h, sg):
                    gg = h * 8 + sg
                    gi_b = h * 16 + 2 * sg + 1
                    e.wait_ge(sTC, 32 * (gi_b + 1))
                    if gg >= 1:
                        e.wait_ge(sAO, gg)
                    for uc in range(16):
                        va = vhi[:, uc * 144 + h * 72: uc * 144 + h * 72 + 65]
                        vb = vlo[:, uc * 144 + h * 72: uc * 144 + h * 72 + 65]
                        for sb2 in range(2):
                            out = pAV[sb2]
                            e.matmul(out, va, pt_s(pth, sb2, uc),
                                     start=(uc == 0), stop=False)
                            e.matmul(out, va, pt_s(ptl, sb2, uc), start=False, stop=False)
                            e.matmul(out, vb, pt_s(pth, sb2, uc), start=False,
                                     stop=(uc == 15))
                    e.inc(sAV, 1)

                def emit_trcopies(e, gp):
                    sb2p = (gp % NSB) % 2
                    for j in range(32):
                        B = 32 * gp + j
                        uc = j % 16
                        e.wait_ge(sTR, B + 1)
                        dst = pth if j < 16 else ptl
                        e.tensor_copy(pt_s(dst, sb2p, uc), pTR[B % 2])
                        e.inc(sTC, 1)

                def vector_av_tail(e, gg):
                    h, sg = divmod(gg, 8)
                    r0 = h * 64
                    e.wait_ge(sAV, gg + 1)
                    e.tensor_copy(aot[:, 0:128], pAV[0][0:64, :])
                    e.tensor_copy(aot[:, 128:256], pAV[1][0:64, :])
                    e.inc(sAO, 1)
                    e.wait_ge(sAH, gg + 1)
                    e.tensor_sub(aol0, aot, aoh0)
                    e.tensor_copy(aoh[r0:r0 + 64, sg * 256:(sg + 1) * 256], aoh0)
                    e.tensor_copy(aol[r0:r0 + 64, sg * 256:(sg + 1) * 256], aol0)
                    e.inc(sAL, 1)

                def scalar_ao(e, gg):
                    e.wait_ge(sAO, gg + 1)
                    e.copy(out=aoh0, in_=aot)
                    e.inc(sAH, 1)

                def group_done(gi):
                    if gi % 2 == 1:
                        return (gi // NSB) * 8 + (gi % NSB) // 2
                    return None

                @blk.tensor
                def _(e):
                    e = Eng(e)
                    clears(e, ALLS, pn)
                    for gi in range(NGI):
                        h, sb = divmod(gi, NSB)
                        r0 = h * 64
                        if gi >= 1:
                            e.wait_ge(sEX, 4 * gi)
                        for ut in range(4):
                            lq_h = qhi[r0:r0 + 64, sb * 128:(sb + 1) * 128]
                            lq_l = qlo[r0:r0 + 64, sb * 128:(sb + 1) * 128]
                            mm3(e, pSC[ut],
                                lq_h, lq_l,
                                khi[r0:r0 + 64, ut * TT:(ut + 1) * TT],
                                klo[r0:r0 + 64, ut * TT:(ut + 1) * TT],
                                True, True)
                            e.inc(sSC, 1)
                        if gi >= 1:
                            emit_tr(e, gi - 1)
                            gg = group_done(gi - 1)
                            if gg is not None:
                                emit_av(e, gg // 8, gg % 8)
                        if gi == NGI - 1:
                            emit_tr(e, gi)
                            emit_av(e, 1, 7)
                    e.wait_ge(sAL, NGG)

                @blk.vector
                def _(e):
                    e = Eng(e, auto=True)
                    join(e, pn)
                    for gi in range(NGI):
                        e.wait_ge(sSC, 4 * gi + 4)
                        for ut in range(4):
                            e.reduce_max(mneg[:, 1 + ut:2 + ut], pSC[ut], axis=AX.X)
                        e.reduce_max(mneg[:, 0:1], mneg[:, 1:5], axis=AX.X, negate=True)
                        e.inc(sMX, 1)
                        if gi >= 1:
                            emit_trcopies(e, gi - 1)
                        e.wait_ge(sEX, 4 * gi + 4)
                        e.reduce_sum(mneg[:, 5:6], ebuf, axis=AX.X)
                        e.drain()
                        e.reciprocal(mneg[:, 6:7], mneg[:, 5:6])
                        e.drain()
                        e.tensor_scalar(ebuf2, ebuf, mneg[:, 6:7], None, ALU.mult)
                        e.inc(sE2, 1)
                        e.wait_ge(sEH, gi + 1)
                        if gi >= 1:
                            e.wait_ge(sTR, 32 * gi)
                        e.tensor_sub(elo, ebuf2, ehi)
                        e.inc(sEL, 1)
                        if gi >= 1:
                            gg = group_done(gi - 1)
                            if gg is not None:
                                vector_av_tail(e, gg)
                        if gi == NGI - 1:
                            emit_trcopies(e, gi)
                            vector_av_tail(e, 15)

                @blk.scalar
                def _(e):
                    e = Eng(e, auto=True)
                    join(e, pn)
                    for gi in range(NGI):
                        e.wait_ge(sMX, gi + 1)
                        if gi >= 1:
                            e.wait_ge(sE2, gi)
                        for ut in range(4):
                            e.activation(ebuf[:, ut * TT:(ut + 1) * TT],
                                         pSC[ut], AF.Exp, bias=mneg[:, 0:1])
                            e.inc(sEX, 1)
                        e.wait_ge(sE2, gi + 1)
                        if gi >= 1:
                            e.wait_ge(sTR, 32 * gi)
                        e.copy(out=ehi, in_=ebuf2)
                        e.inc(sEH, 1)
                        gg = group_done(gi - 1) if gi >= 1 else None
                        if gg is not None:
                            scalar_ao(e, gg)
                        if gi == NGI - 1:
                            scalar_ao(e, 15)
                    e.wait_ge(sAL, NGG)
                    e.dma_start(out=aohs[pair], in_=aoh).then_inc(sSPL, 16)
                    e.dma_start(out=aols[pair], in_=aol).then_inc(sSPL, 16)
                    e.wait_ge(sSPL, 32)

    def oproj(b):
        with ExitStack() as st, nc.Block() as blk:
            sem = {n: st.enter_context(sem_t(n)) for n in
                   "sWO sWOH sWOL sOC sOE".split()}
            sX12 = [st.enter_context(sem_t("sX1")) for _ in range(2)]
            sAT2 = [st.enter_context(sem_t("sAT")) for _ in range(2)]
            ALLS = list(sem.values()) + sX12 + sAT2
            pn = open_phase()
            sWO, sWOH, sWOL = sem["sWO"], sem["sWOH"], sem["sWOL"]
            sOC, sOE = sem["sOC"], sem["sOE"]

            pO = [st.enter_context(psum_t(f"pO{i}", [128, TT], F32)).ap()
                  for i in range(2)]

            big = st.enter_context(sbuf_t("big", [128, DC * TT], F32)).ap()
            woh = st.enter_context(sbuf_t("woh", [128, DC * D], BF16)).ap()
            wol = st.enter_context(sbuf_t("wol", [128, DC * D], BF16)).ap()
            x1b = st.enter_context(sbuf_t("x1b", [128, 2 * DC * TT], F32)).ap()
            ath = st.enter_context(sbuf_t("ath", [128, 2 * DC * TT], BF16)).ap()
            atl = st.enter_context(sbuf_t("atl", [128, 2 * DC * TT], BF16)).ap()

            def xs(x, k):
                return x[:, (k % 2) * DC * TT:(k % 2 + 1) * DC * TT]

            @blk.sync
            def _(e):
                e = Eng(e)
                clears(e, ALLS, pn)
                for hf in range(2):
                    if hf == 1:
                        e.wait_ge(sWOL, 1)
                    e.dma_start(
                        out=big.rearrange("p (c o) -> p c o", c=4),
                        in_=woT.rearrange("(c p) o -> p c o", p=128)[:, hf * 4:(hf + 1) * 4, :],
                    ).then_inc(sWO, 16)
                for tt in range(NT):
                    if tt >= 2:
                        e.wait_ge(sOE, 8 * (tt - 1))
                    e.dma_start(out=xs(x1b, tt).rearrange("p (c t) -> p c t", c=DC),
                                in_=x1s[b][:, :, tt * TT:(tt + 1) * TT].rearrange("c p t -> p c t"),
                                ).then_inc(sX12[tt % 2], 16)
                    for pr in range(NPAIR):
                        e.dma_start(out=xs(ath, tt)[:, pr * TT:(pr + 1) * TT],
                                    in_=aohs[pr][:, tt * TT:(tt + 1) * TT]).then_inc(sAT2[tt % 2], 16)
                        e.dma_start(out=xs(atl, tt)[:, pr * TT:(pr + 1) * TT],
                                    in_=aols[pr][:, tt * TT:(tt + 1) * TT]).then_inc(sAT2[tt % 2], 16)

            @blk.scalar
            def _(e):
                e = Eng(e, auto=True)
                join(e, pn)
                for hf in range(2):
                    e.wait_ge(sWO, 16 * (hf + 1))
                    e.copy(out=woh[:, hf * DC * TT:(hf + 1) * DC * TT], in_=big)
                    e.inc(sWOH, 1)

            @blk.vector
            def _(e):
                e = Eng(e, auto=True)
                join(e, pn)
                for hf in range(2):
                    e.wait_ge(sWOH, hf + 1)
                    e.tensor_sub(wol[:, hf * DC * TT:(hf + 1) * DC * TT], big,
                                 woh[:, hf * DC * TT:(hf + 1) * DC * TT])
                    e.inc(sWOL, 1)
                for tt in range(NT):
                    for oc in range(DC):
                        j = tt * DC + oc
                        e.wait_ge(sOC, j + 1)
                        col = oc * S_LEN + tt * TT
                        e.tensor_add(acc[:, col:col + TT],
                                     xs(x1b, tt)[:, oc * TT:(oc + 1) * TT],
                                     pO[j % 2])
                        e.inc(sOE, 1)

            @blk.tensor
            def _(e):
                e = Eng(e)
                join(e, pn)
                e.wait_ge(sWOL, 2)
                for tt in range(NT):
                    e.wait_ge(sX12[tt % 2], 16 * (tt // 2 + 1))
                    e.wait_ge(sAT2[tt % 2], 256 * (tt // 2 + 1))
                    for oc in range(DC):
                        j = tt * DC + oc
                        if j >= 2:
                            e.wait_ge(sOE, j - 1)
                        for hc in range(DC):
                            co = hc * D + oc * 128
                            mm3(e, pO[j % 2],
                                woh[:, co:co + 128], wol[:, co:co + 128],
                                xs(ath, tt)[:, hc * TT:(hc + 1) * TT],
                                xs(atl, tt)[:, hc * TT:(hc + 1) * TT],
                                hc == 0, hc == DC - 1)
                        e.inc(sOC, 1)
                e.wait_ge(sOE, NT * DC)

    out_sem_cm = sem_t("sOUT")
    sOUT = out_sem_cm.__enter__()

    n_stores = [0]

    def store_out(b):
        n_stores[0] += 1
        with nc.Block() as blk:
            @blk.sync
            def _(e):
                e = Eng(e)
                e.dma_start(out=y[b].rearrange("(c p) t -> p c t", p=128),
                            in_=acc.rearrange("p (c t) -> p c t", c=DC)).then_inc(sOUT, 16)

    step = itertools.count()

    def go():
        return next(step) < limit

    for b in range(NB):
        if go():
            ffn_stageA(xT[b])
        if go():
            ffn_stageB()
        if go():
            spill_acc(x1s[b].rearrange("c p t -> p c t"))
    for b in range(NB):
        if go():
            attn_ln(b)
        for pair in range(NPAIR):
            if go():
                attn_pair(pair)
        if go():
            oproj(b)
        if go():
            ffn_stageA(None)
        if go():
            ffn_stageB()
        if go():
            store_out(b)

    with nc.Block() as blk:
        @blk.sync
        def _(e):
            e = Eng(e)
            if n_stores[0]:
                e.wait_ge(sOUT, 16 * n_stores[0])
    out_sem_cm.__exit__(None, None, None)
    gP_cm.__exit__(None, None, None)
    return nc



_prog_cache = {}


def _tables():
    hd = 64
    inv = (1.0 / (np.float32(10000.0) ** (np.arange(0, hd, 2, dtype=np.float32) / np.float32(hd)))).astype(np.float32)
    t = np.arange(S_LEN, dtype=np.float32)
    freqs = (t[:, None] * inv[None, :]).astype(np.float32)
    emb = np.concatenate([freqs, freqs], axis=1)
    s8 = np.sqrt(np.float64(8.0))
    c64 = np.cos(emb.astype(np.float64)) * s8
    s64 = np.sin(emb.astype(np.float64)) * s8
    ch = c64.astype(np.float32)
    cl = (c64 - ch.astype(np.float64)).astype(np.float32)
    sh = s64.astype(np.float32)
    sl = (s64 - sh.astype(np.float64)).astype(np.float32)
    def dup(a):
        aT = np.ascontiguousarray(a.T)
        return np.ascontiguousarray(np.concatenate([aT, aT], axis=0))
    P = np.zeros((64, 64), dtype=np.float32)
    for i in range(32):
        P[i, 2 * i + 1] = -1.0
        P[32 + i, 2 * i] = 1.0
    P128 = np.zeros((128, 128), dtype=np.float32)
    P128[0:64, 0:64] = P
    P128[64:128, 64:128] = P
    return dup(ch), dup(cl), dup(sh), dup(sl), np.ascontiguousarray(P128.T)


def _indicator():
    a = np.zeros((8, 8 * 128), dtype=np.float32)
    for e in range(8):
        a[e, e * 128:(e + 1) * 128] = 1.0
    return a


def kernel(L0_attn_hidden_states, L0_attn_Wq, L0_attn_Wk, L0_attn_Wv, L0_attn_Wo,
           L0_ffn_router_w, L0_ffn_router_b, L0_ffn_routed_w1, L0_ffn_routed_w3,
           L0_ffn_routed_w2, L0_ffn_shared_w1, L0_ffn_shared_w3, L0_ffn_shared_w2):
    x = np.asarray(L0_attn_hidden_states, dtype=np.float32)
    BTOT = x.shape[0]
    ncores = 8
    xTfull = np.ascontiguousarray(x.transpose(0, 2, 1))

    ch, cl, sh, sl, p128T = _tables()
    common = {
        "wqT": np.ascontiguousarray(np.asarray(L0_attn_Wq, np.float32).T),
        "wkT": np.ascontiguousarray(np.asarray(L0_attn_Wk, np.float32).T),
        "wvT": np.ascontiguousarray(np.asarray(L0_attn_Wv, np.float32).T),
        "woT": np.ascontiguousarray(np.asarray(L0_attn_Wo, np.float32).T),
        "w1": np.ascontiguousarray(np.concatenate([L0_ffn_routed_w1, L0_ffn_shared_w1], 0).astype(np.float32)),
        "w3": np.ascontiguousarray(np.concatenate([L0_ffn_routed_w3, L0_ffn_shared_w3], 0).astype(np.float32)),
        "w2": np.ascontiguousarray(np.concatenate([L0_ffn_routed_w2, L0_ffn_shared_w2], 0).astype(np.float32)),
        "rw": np.ascontiguousarray(np.pad(np.asarray(L0_ffn_router_w, np.float32),
                                           ((0, 0), (0, 120)))),
        "rbias": np.ascontiguousarray(np.asarray(L0_ffn_router_b, np.float32)).reshape(1, 8),
        "tch": ch, "tcl": cl, "tsh": sh, "tsl": sl, "p128T": p128T,
        "ident_b": np.eye(128, dtype=np.float32).astype(ml_dtypes.bfloat16),
        "ident_f": np.eye(128, dtype=np.float32),
        "inds": _indicator(),
    }

    if "nc" not in _prog_cache:
        _prog_cache["nc"] = build_program()
    nc = _prog_cache["nc"]

    in_maps = []
    for c in range(ncores):
        m = dict(common)
        m["xT"] = np.ascontiguousarray(xTfull[NB * c:NB * (c + 1)])
        in_maps.append(m)

    res = run_bass_kernel_spmd(nc, in_maps, core_ids=list(range(ncores)))
    out = np.empty((BTOT, S_LEN, D), dtype=np.float32)
    for c in range(ncores):
        yc = res.results[c]["y"]
        for j in range(NB):
            out[NB * c + j] = yc[j].T
    return out


# version 10 (final)
# speedup vs baseline: 1.0115x; 1.0115x over previous
import numpy as np
import ml_dtypes
import concourse.bass as bass
import concourse.mybir as mybir
from concourse.bass_utils import run_bass_kernel_spmd
from contextlib import ExitStack

F32 = mybir.dt.float32
BF16 = mybir.dt.bfloat16
AF = mybir.ActivationFunctionType
ALU = mybir.AluOpType
AX = mybir.AxisListType

NB = 2
S_LEN = 2048
D = 1024
DC = 8
NT = 4
TT = 512
HH = 4
NE = 10
NPAIR = 8
NSB = 16
EPS = 1e-5


def build_program(limit=10 ** 9):
    import itertools
    nc = bass.Bass("TRN2", target_bir_lowering=False, detect_race_conditions=False)
    _uid = itertools.count()

    def sbuf_t(name, shape, dtype):
        return nc.sbuf_tensor(f"{name}_{next(_uid)}", shape, dtype)

    def psum_t(name, shape, dtype=F32):
        return nc.psum_tensor(f"{name}_{next(_uid)}", shape, dtype)

    def sem_t(name):
        return nc.semaphore(f"{name}_{next(_uid)}")

    _NO_DRAIN = {"dma_start", "nop", "wait_ge", "sem_clear", "sem_inc", "drain",
                 "then_inc", "wait_op"}

    class Eng:
        def __init__(self, raw, auto=False):
            object.__setattr__(self, "_raw", raw)
            object.__setattr__(self, "_auto", auto)
            object.__setattr__(self, "last", None)
        def __getattr__(self, name):
            f = getattr(self._raw, name)
            if not callable(f):
                return f
            def call(*a, **k):
                r = f(*a, **k)
                object.__setattr__(self, "last", r)
                if self._auto and name not in _NO_DRAIN:
                    self._raw.drain()
                return r
            return call
        def inc(self, sem, n=1):
            self._raw.drain()
            self._raw.nop().then_inc(sem, n)

        incd = inc

    gP_cm = nc.semaphore("gPhase")
    gP = gP_cm.__enter__()
    phase_no = [0]

    def open_phase():
        phase_no[0] += 1
        return phase_no[0]

    def clears(e, sems_list, pn):
        for s in sems_list:
            e.sem_clear(s)
        e.sem_inc(gP, 1)

    def join(e, pn):
        e.wait_ge(gP, pn)

    def _contig_groups(sems_list):
        ss = sorted(sems_list, key=lambda s: s.num)
        groups = []
        for s in ss:
            if groups and s.num == groups[-1][-1].num + 1 and len(groups[-1]) < 4:
                groups[-1].append(s)
            else:
                groups.append([s])
        return groups

    xT = nc.dram_tensor("xT", [NB, D, S_LEN], F32, kind="ExternalInput").ap()
    wqT = nc.dram_tensor("wqT", [D, D], F32, kind="ExternalInput").ap()
    wkT = nc.dram_tensor("wkT", [D, D], F32, kind="ExternalInput").ap()
    wvT = nc.dram_tensor("wvT", [D, D], F32, kind="ExternalInput").ap()
    woT = nc.dram_tensor("woT", [D, D], F32, kind="ExternalInput").ap()
    w1 = nc.dram_tensor("w1", [NE, D, 512], F32, kind="ExternalInput").ap()
    w3 = nc.dram_tensor("w3", [NE, D, 512], F32, kind="ExternalInput").ap()
    w2 = nc.dram_tensor("w2", [NE, 512, D], F32, kind="ExternalInput").ap()
    rw = nc.dram_tensor("rw", [D, 128], F32, kind="ExternalInput").ap()
    rbias = nc.dram_tensor("rbias", [1, 8], F32, kind="ExternalInput").ap()
    tch = nc.dram_tensor("tch", [128, S_LEN], F32, kind="ExternalInput").ap()
    tcl = nc.dram_tensor("tcl", [128, S_LEN], F32, kind="ExternalInput").ap()
    tsh = nc.dram_tensor("tsh", [128, S_LEN], F32, kind="ExternalInput").ap()
    tsl = nc.dram_tensor("tsl", [128, S_LEN], F32, kind="ExternalInput").ap()
    p128T = nc.dram_tensor("p128T", [128, 128], F32, kind="ExternalInput").ap()
    ident_b = nc.dram_tensor("ident_b", [128, 128], BF16, kind="ExternalInput").ap()
    ident_f = nc.dram_tensor("ident_f", [128, 128], F32, kind="ExternalInput").ap()
    inds_d = nc.dram_tensor("inds", [8, 8 * 128], F32, kind="ExternalInput").ap()

    y = nc.dram_tensor("y", [NB, D, S_LEN], F32, kind="ExternalOutput").ap()

    x1s = nc.dram_tensor("x1s", [NB, DC, 128, S_LEN], F32, kind="Internal").ap()
    xnhs = nc.dram_tensor("xnhs", [DC, 128, S_LEN], BF16, kind="Internal").ap()
    xnls = nc.dram_tensor("xnls", [DC, 128, S_LEN], BF16, kind="Internal").ap()
    aohs = nc.dram_tensor("aohs", [NPAIR, 128, S_LEN], BF16, kind="Internal").ap()
    aols = nc.dram_tensor("aols", [NPAIR, 128, S_LEN], BF16, kind="Internal").ap()
    tap = nc.dram_tensor("tap", [128, S_LEN], F32, kind="ExternalOutput").ap()

    acc = nc.alloc_sbuf_tensor("acc", [128, DC * S_LEN], F32).ap()
    scT = nc.alloc_sbuf_tensor("scT", [8, S_LEN], F32).ap()
    rw_hi = nc.alloc_sbuf_tensor("rw_hi", [128, DC * 128], BF16).ap()
    rw_lo = nc.alloc_sbuf_tensor("rw_lo", [128, DC * 128], BF16).ap()
    rb_b = nc.alloc_sbuf_tensor("rb_b", [128, 8], F32).ap()
    ones1 = nc.alloc_sbuf_tensor("ones1", [1, 128], F32).ap()
    onescol = nc.alloc_sbuf_tensor("onescol", [128, 1], F32).ap()
    identb_s = nc.alloc_sbuf_tensor("identb_s", [128, 128], BF16).ap()
    identf_s = nc.alloc_sbuf_tensor("identf_s", [128, 128], F32).ap()
    consts = nc.alloc_sbuf_tensor("consts", [128, 8], F32).ap()
    epsrow = consts[0:1, 0:1]
    negbig = consts[:, 1:2]
    rwf = nc.alloc_sbuf_tensor("rwf", [128, DC * 128], F32).ap()
    inds_s = nc.alloc_sbuf_tensor("inds_s", [8, 8 * 128], F32).ap()
    rb_row = nc.alloc_sbuf_tensor("rb_row", [1, 8], F32).ap()

    def mm3(t, out_ap, lhi, llo, rhi, rlo, first, last):
        t.matmul(out_ap, lhi, rhi, start=first, stop=False)
        t.matmul(out_ap, lhi, rlo, start=False, stop=False)
        t.matmul(out_ap, llo, rhi, start=False, stop=last)

    with sem_t("c0") as c0, sem_t("c1") as c1, \
            sbuf_t("rwf", [128, DC * 128], F32) as rwf_t, nc.Block() as blk:
        rwf = rwf_t.ap()
        @blk.sync
        def _(e):
            e = Eng(e)
            e.dma_start(out=identb_s, in_=ident_b).then_inc(c0, 16)
            e.dma_start(out=identf_s, in_=ident_f).then_inc(c0, 16)
            e.dma_start(out=rwf.rearrange("p (c e) -> p c e", c=DC),
                        in_=rw.rearrange("(c p) e -> p c e", p=128)).then_inc(c0, 16)
            e.dma_start(out=rb_row, in_=rbias).then_inc(c0, 16)
            e.dma_start(out=inds_s, in_=inds_d).then_inc(c0, 16)

        @blk.scalar
        def _(e):
            e = Eng(e)
            e.wait_ge(c0, 80)
            e.copy(out=rw_hi, in_=rwf)
            e.inc(c1, 1)

        @blk.vector
        def _(e):
            e = Eng(e, auto=True)
            e.memset(ones1, 1.0)
            e.memset(onescol, 1.0)
            e.memset(consts[:, 0:1], EPS)
            e.memset(consts[:, 1:2], -1e30)
            e.memset(consts[:, 2:3], -0.5)
            e.memset(consts[:, 3:4], 1.5)
            e.wait_ge(c1, 1)
            e.tensor_sub(rw_lo, rwf, rw_hi)

    with sem_t("c2") as c2, nc.Block() as blk:
        with psum_t("prb", [128, 8], F32) as prb_t:
            prb = prb_t.ap()

            @blk.tensor
            def _(e):
                e = Eng(e)
                e.matmul(prb, ones1, rb_row, start=True, stop=True)
                e.inc(c2, 1)

            @blk.vector
            def _(e):
                e = Eng(e, auto=True)
                e.wait_ge(c2, 1)
                e.tensor_copy(rb_b, prb)

    def ffn_stageA(xsrc):
        with ExitStack() as st, nc.Block() as blk:
            sem = {n: st.enter_context(sem_t(n)) for n in
                   ("sSQ sS1 sEp1 sEp2 sEp3 sBC sXF sXH sXN sRT "
                    "sG1 sG2 sG3 sG4 sGV2 sGT sGF").split()}
            sX2 = [st.enter_context(sem_t("sX")) for _ in range(2)]
            sSP2 = [st.enter_context(sem_t("sSP")) for _ in range(2)]
            ALLS = list(sem.values()) + sX2 + sSP2
            pn = open_phase()
            sSQ, sS1 = sem["sSQ"], sem["sS1"]
            sEp1, sEp2, sEp3, sBC = sem["sEp1"], sem["sEp2"], sem["sEp3"], sem["sBC"]
            sXF, sXH, sXN, sRT = sem["sXF"], sem["sXH"], sem["sXN"], sem["sRT"]
            sG1, sG2, sG3, sG4 = sem["sG1"], sem["sG2"], sem["sG3"], sem["sG4"]
            sGV2, sGT, sGF = sem["sGV2"], sem["sGT"], sem["sGF"]

            pLN1 = st.enter_context(psum_t("pLN1", [1, TT], F32)).ap()
            pLN2 = st.enter_context(psum_t("pLN2", [1, TT], F32)).ap()
            pB1 = st.enter_context(psum_t("pB1", [128, TT], F32)).ap()
            pB2 = st.enter_context(psum_t("pB2", [128, TT], F32)).ap()
            pRT = [st.enter_context(psum_t(f"pRT{i}", [128, 128], F32)).ap()
                   for i in range(2)]
            pGT = [st.enter_context(psum_t(f"pGT{i}", [8, 128], F32)).ap()
                   for i in range(2)]

            xnh = st.enter_context(sbuf_t("xnh", [128, 2 * DC * TT], BF16)).ap()
            xnl = st.enter_context(sbuf_t("xnl", [128, 2 * DC * TT], BF16)).ap()
            big = st.enter_context(sbuf_t("big", [128, DC * TT], F32)).ap()
            lnrow = st.enter_context(sbuf_t("lnrow", [1, 12 * TT], F32)).ap()

            def lnr(i):
                return lnrow[0:1, i * TT:(i + 1) * TT]
            lg = st.enter_context(sbuf_t("lg", [128, 8], F32)).ap()
            eg0 = st.enter_context(sbuf_t("eg0", [128, 8], F32)).ap()
            gsc = st.enter_context(sbuf_t("gsc", [128, 8], F32)).ap()
            grow = st.enter_context(sbuf_t("grow", [128, 32], F32)).ap()

            def accv(tt, dc):
                return acc[:, dc * S_LEN + tt * TT: dc * S_LEN + (tt + 1) * TT]

            def xnv(x, tt, dc):
                base = (tt % 2) * DC * TT
                return x[:, base + dc * TT: base + (dc + 1) * TT]

            def xslot(x, tt):
                base = (tt % 2) * DC * TT
                return x[:, base: base + DC * TT]

            @blk.sync
            def _(e):
                e = Eng(e)
                clears(e, ALLS, pn)
                if xsrc is not None:
                    for tt in range(NT):
                        if tt >= 2:
                            e.wait_ge(sSQ, tt - 1)
                            e.wait_ge(sS1, tt - 1)
                        e.dma_start(
                            out=acc.rearrange("p (c t) -> p c t", c=DC)[:, :, tt * TT:(tt + 1) * TT],
                            in_=xsrc.rearrange("(c p) t -> p c t", p=128)[:, :, tt * TT:(tt + 1) * TT],
                        ).then_inc(sX2[tt % 2], 16)

            @blk.vector
            def _(e):
                e = Eng(e, auto=True)
                join(e, pn)
                for tt in range(NT):
                    if xsrc is not None:
                        e.wait_ge(sX2[tt % 2], 16 * (tt // 2 + 1))
                    for dc in range(DC):
                        e.tensor_mul(big[:, dc * TT:(dc + 1) * TT], accv(tt, dc), accv(tt, dc))
                    e.inc(sSQ, 1)
                    e.wait_ge(sEp1, 1 + 2 * tt)
                    e.tensor_mul(lnr(2), lnr(0), lnr(0))
                    e.tensor_sub(lnr(3), lnr(1), lnr(2))
                    e.tensor_scalar_add(lnr(4), lnr(3), epsrow)
                    e.inc(sEp2, 1)
                    e.wait_ge(sEp1, 2 + 2 * tt)
                    e.reciprocal(lnr(6), lnr(5))
                    e.tensor_mul(lnr(7), lnr(6), lnr(6))
                    e.tensor_mul(lnr(8), lnr(7), lnr(4))
                    e.tensor_scalar(lnr(9), lnr(8), consts[0:1, 2:3], consts[0:1, 3:4],
                                    ALU.mult, ALU.add)
                    e.tensor_mul(lnr(10), lnr(6), lnr(9))
                    e.tensor_mul(lnr(11), lnr(0), lnr(10))
                    e.inc(sEp3, 1)
                    e.wait_ge(sBC, tt + 1)
                    if tt >= 2:
                        e.wait_ge(sSP2[tt % 2], 32 * (tt // 2))
                        e.wait_ge(sRT, 4 * (tt - 1))
                    for dc in range(DC):
                        e.tensor_mul(big[:, dc * TT:(dc + 1) * TT], accv(tt, dc), pB1)
                        e.tensor_sub(big[:, dc * TT:(dc + 1) * TT],
                                     big[:, dc * TT:(dc + 1) * TT], pB2)
                    e.inc(sXF, 1)
                    e.wait_ge(sXH, tt + 1)
                    for dc in range(DC):
                        e.tensor_sub(xnv(xnl, tt, dc), big[:, dc * TT:(dc + 1) * TT],
                                     xnv(xnh, tt, dc))
                    e.inc(sXN, 1)
                    for tc in range(4):
                        k = 4 * tt + tc
                        e.wait_ge(sRT, k + 1)
                        if k >= 1:
                            e.wait_ge(sGT, k)
                        e.tensor_add(lg, pRT[k % 2][:, 0:8], rb_b)
                        e.reduce_max(gsc[:, 0:1], lg, axis=AX.X, negate=True)
                        e.reduce_max(gsc[:, 1:2], lg, axis=AX.X)
                        e.drain()
                        e.tensor_scalar(grow[:, 0:8], lg, gsc[:, 1:2], None, ALU.subtract)
                        e.inc(sG1, 1)
                        e.wait_ge(sG2, k + 1)
                        e.scalar_tensor_tensor(grow[:, 8:16], grow[:, 0:8], negbig, lg,
                                               ALU.mult, ALU.add)
                        e.reduce_max(gsc[:, 2:3], grow[:, 8:16], axis=AX.X)
                        e.drain()
                        e.tensor_scalar(grow[:, 16:24], lg, gsc[:, 2:3], None, ALU.subtract)
                        e.inc(sG3, 1)
                        e.wait_ge(sG4, k + 1)
                        e.tensor_scalar_min(grow[:, 24:32], grow[:, 16:24], 1.0)
                        e.tensor_mul(grow[:, 0:8], eg0, grow[:, 24:32])
                        e.reduce_sum(gsc[:, 3:4], grow[:, 0:8], axis=AX.X)
                        e.drain()
                        e.reciprocal(gsc[:, 4:5], gsc[:, 3:4])
                        e.drain()
                        e.tensor_scalar(lg, grow[:, 0:8], gsc[:, 4:5], None, ALU.mult)
                        e.inc(sGV2, 1)

            @blk.scalar
            def _(e):
                e = Eng(e)
                join(e, pn)
                for tt in range(NT):
                    e.wait_ge(sS1, tt + 1)
                    e.activation(lnr(0), pLN1, AF.Copy, scale=1.0 / D)
                    e.activation(lnr(1), pLN2, AF.Copy, scale=1.0 / D)
                    e.inc(sEp1, 1)
                    e.wait_ge(sEp2, tt + 1)
                    e.activation(lnr(5), lnr(4), AF.Sqrt)
                    e.inc(sEp1, 1)
                    e.wait_ge(sXF, tt + 1)
                    if tt >= 2:
                        e.wait_ge(sRT, 4 * (tt - 1))
                    for dc in range(DC):
                        e.copy(out=xnv(xnh, tt, dc), in_=big[:, dc * TT:(dc + 1) * TT])
                    e.inc(sXH, 1)
                    e.wait_ge(sXN, tt + 1)
                    e.dma_start(
                        out=xnhs.rearrange("c p t -> p c t")[:, :, tt * TT:(tt + 1) * TT],
                        in_=xslot(xnh, tt).rearrange("p (c t) -> p c t", c=DC),
                    ).then_inc(sSP2[tt % 2], 16)
                    e.dma_start(
                        out=xnls.rearrange("c p t -> p c t")[:, :, tt * TT:(tt + 1) * TT],
                        in_=xslot(xnl, tt).rearrange("p (c t) -> p c t", c=DC),
                    ).then_inc(sSP2[tt % 2], 16)
                    for tc in range(4):
                        k = 4 * tt + tc
                        e.wait_ge(sG1, k + 1)
                        e.activation(grow[:, 8:16], grow[:, 0:8], AF.Sign)
                        e.drain()
                        e.add(out=grow[:, 0:8], in_=grow[:, 8:16], add=1.0)
                        e.activation(eg0, lg, AF.Exp, bias=gsc[:, 0:1])
                        e.inc(sG2, 1)
                        e.wait_ge(sG3, k + 1)
                        e.activation(grow[:, 24:32], grow[:, 16:24], AF.Sign)
                        e.drain()
                        e.add(out=grow[:, 16:24], in_=grow[:, 24:32], add=1.0)
                        e.inc(sG4, 1)
                        e.wait_ge(sGT, k + 1)
                        import os as _os
                        if _os.environ.get("KSCT"):
                            e.memzero(scT[0:8, k * 128:(k + 1) * 128])
                        else:
                            e.activation(out=scT[0:8, k * 128:(k + 1) * 128],
                                         in_=pGT[k % 2], func=AF.Copy, scale=0.5)
                        e.inc(sGF, 1)
                import os as _os2
                if _os2.environ.get("KTAP"):
                    e.wait_ge(sGF, 16)
                    e.dma_start(out=tap[0:8, :], in_=scT).then_inc(sGF, 16)
                    e.dma_start(out=tap[:, 256:264], in_=lg).then_inc(sGF, 16)
                    e.dma_start(out=tap[:, 288:320], in_=grow).then_inc(sGF, 16)
                    e.dma_start(out=tap[:, 320:328], in_=gsc).then_inc(sGF, 16)
                    e.dma_start(out=tap[:, 328:336], in_=eg0).then_inc(sGF, 16)
                    e.dma_start(out=tap[:, 336:344], in_=rb_b).then_inc(sGF, 16)
                    e.wait_ge(sGF, 112)

            @blk.tensor
            def _(e):
                e = Eng(e)
                join(e, pn)
                for tt in range(NT):
                    if xsrc is not None:
                        e.wait_ge(sX2[tt % 2], 16 * (tt // 2 + 1))
                    e.wait_ge(sSQ, tt + 1)
                    if tt >= 1:
                        e.wait_ge(sEp1, 2 * tt - 1)
                    for dc in range(DC):
                        e.matmul(pLN1, onescol, accv(tt, dc),
                                 start=(dc == 0), stop=(dc == DC - 1))
                    for dc in range(DC):
                        e.matmul(pLN2, onescol, big[:, dc * TT:(dc + 1) * TT],
                                 start=(dc == 0), stop=(dc == DC - 1))
                    e.inc(sS1, 1)
                    e.wait_ge(sEp3, tt + 1)
                    if tt >= 1:
                        e.wait_ge(sXF, tt)
                    e.matmul(pB1, ones1, lnr(10), start=True, stop=True)
                    e.matmul(pB2, ones1, lnr(11), start=True, stop=True)
                    e.inc(sBC, 1)
                    e.wait_ge(sXN, tt + 1)
                    for tc in range(4):
                        k = 4 * tt + tc
                        if k >= 2:
                            e.wait_ge(sG1, k - 1)
                        xh = xslot(xnh, tt)
                        xl = xslot(xnl, tt)
                        for dc in range(DC):
                            c0_ = dc * TT + tc * 128
                            mm3(e, pRT[k % 2],
                                xh[:, c0_:c0_ + 128], xl[:, c0_:c0_ + 128],
                                rw_hi[:, dc * 128:(dc + 1) * 128],
                                rw_lo[:, dc * 128:(dc + 1) * 128],
                                dc == 0, dc == DC - 1)
                        e.inc(sRT, 1)
                        e.wait_ge(sGV2, k + 1)
                        if k >= 2:
                            e.wait_ge(sGF, k - 1)
                        e.matmul(pGT[k % 2], lg, identf_s, start=True, stop=True)
                        e.inc(sGT, 1)
                e.wait_ge(sSP2[0], 64)
                e.wait_ge(sSP2[1], 64)
                e.wait_ge(sGF, 16)

    def ffn_stageB():
        import os
        NE_RUN = int(os.environ.get("KNE", NE))
        KNOACC = os.environ.get("KNOACC")
        KNOSILU = os.environ.get("KNOSILU")
        KNOH = os.environ.get("KNOH")
        with ExitStack() as st, nc.Block() as blk:
            sem = {n: st.enter_context(sem_t(n)) for n in
                   "sWF sWL sGB sGBE sH sSIL sHM sHH sHL sY sYE".split()}
            sWHc = st.enter_context(sem_t("sWHc"))
            sXT2 = [st.enter_context(sem_t("sXT")) for _ in range(2)]
            ALLS = list(sem.values()) + [sWHc] + sXT2
            pn = open_phase()
            sWF, sWL = sem["sWF"], sem["sWL"]
            sGB, sGBE, sH, sSIL = sem["sGB"], sem["sGBE"], sem["sH"], sem["sSIL"]
            sHM, sHH, sHL = sem["sHM"], sem["sHH"], sem["sHL"]
            sY, sYE = sem["sY"], sem["sYE"]

            ph = [st.enter_context(psum_t(f"ph{i}", [128, TT], F32)).ap()
                  for i in range(4)]
            pY = [st.enter_context(psum_t(f"pY{i}", [128, TT], F32)).ap()
                  for i in range(2)]
            pGB = [st.enter_context(psum_t(f"pGB{i}", [128, TT], F32)).ap()
                   for i in range(2)]

            wr_hi = st.enter_context(sbuf_t("wr_hi", [128, 3 * DC * TT], BF16)).ap()
            wr_lo = st.enter_context(sbuf_t("wr_lo", [128, 3 * DC * TT], BF16)).ap()
            wtmp = st.enter_context(sbuf_t("wtmp", [128, DC * TT], F32)).ap()
            xth = st.enter_context(sbuf_t("xth", [128, 2 * DC * TT], BF16)).ap()
            xtl = st.enter_context(sbuf_t("xtl", [128, 2 * DC * TT], BF16)).ap()
            hhi = st.enter_context(sbuf_t("hhi", [128, HH * TT], BF16)).ap()
            hlo = st.enter_context(sbuf_t("hlo", [128, HH * TT], BF16)).ap()
            sil = st.enter_context(sbuf_t("sil", [128, 2 * TT], F32)).ap()
            hgt = st.enter_context(sbuf_t("hgt", [128, 2 * TT], F32)).ap()
            gbs = st.enter_context(sbuf_t("gbs", [128, 2 * TT], F32)).ap()

            NM = 3 * NE_RUN

            def wslot(x, m):
                return x[:, (m % 3) * DC * TT:(m % 3 + 1) * DC * TT]

            def xs(x, k):
                return x[:, (k % 2) * DC * TT:(k % 2 + 1) * DC * TT]

            def p2(x, j):
                return x[:, (j % 2) * TT:(j % 2 + 1) * TT]

            def w_src(m):
                e_, j = divmod(m, 3)
                t = [w1, w3, w2][j]
                c = DC if j < 2 else HH
                return t[e_].rearrange("(c p) h -> p c h", p=128), c

            @blk.sync
            def _(e):
                e = Eng(e)
                clears(e, ALLS, pn)
                for m in range(3):
                    if m >= 1:
                        e.wait_ge(sWL, m)
                    src3, c = w_src(m)
                    e.dma_start(out=wtmp.rearrange("p (c h) -> p c h", c=c),
                                in_=src3).then_inc(sWF, 16)
                for ee in range(NE_RUN):
                    for tt in range(NT):
                        k = ee * NT + tt
                        if k >= 2:
                            e.wait_ge(sH, 2 * k - 2)
                        e.dma_start(out=xs(xth, k).rearrange("p (c t) -> p c t", c=DC),
                                    in_=xnhs.rearrange("c p t -> p c t")[:, :, tt * TT:(tt + 1) * TT],
                                    ).then_inc(sXT2[k % 2], 16)
                        e.dma_start(out=xs(xtl, k).rearrange("p (c t) -> p c t", c=DC),
                                    in_=xnls.rearrange("c p t -> p c t")[:, :, tt * TT:(tt + 1) * TT],
                                    ).then_inc(sXT2[k % 2], 16)
                    if ee + 1 < NE_RUN:
                        for j in range(3):
                            m = 3 * (ee + 1) + j
                            e.wait_ge(sWL, m)
                            src3, c = w_src(m)
                            e.dma_start(out=wtmp.rearrange("p (c h) -> p c h", c=c),
                                        in_=src3).then_inc(sWF, 16)

            @blk.vector
            def _(e):
                e = Eng(e, auto=True)
                join(e, pn)
                for ee in range(NE_RUN):
                    for j in range(3):
                        m = 3 * ee + j
                        e.wait_ge(sWHc, m + 1)
                        e.tensor_sub(wslot(wr_lo, m), wtmp, wslot(wr_hi, m))
                        e.inc(sWL, 1)
                    for tt in range(NT):
                        k = ee * NT + tt
                        e.wait_ge(sGB, k + 1)
                        if ee < 8 and not os.environ.get("KGB1"):
                            e.tensor_copy(p2(gbs, k), pGB[k % 2])
                        else:
                            e.memset(p2(gbs, k), 0.5)
                        e.inc(sGBE, 1)
                        for half in range(2):
                            kk = 2 * k + half
                            e.wait_ge(sSIL, 2 * kk + 2)
                            for c in range(2):
                                if KNOH:
                                    e.memset(p2(hgt, c), 0.01)
                                elif os.environ.get("KNOGB"):
                                    e.tensor_mul(p2(hgt, c), p2(sil, c), ph[2 + c])
                                elif os.environ.get("KNOPS"):
                                    e.tensor_mul(p2(hgt, c), p2(sil, c), p2(gbs, k))
                                else:
                                    e.tensor_mul(p2(hgt, c), p2(sil, c), ph[2 + c])
                                    e.tensor_mul(p2(hgt, c), p2(hgt, c), p2(gbs, k))
                            e.inc(sHM, 1)
                            e.wait_ge(sHH, kk + 1)
                            for c in range(2):
                                hc = 2 * half + c
                                e.tensor_sub(hlo[:, hc * TT:(hc + 1) * TT],
                                             p2(hgt, c), hhi[:, hc * TT:(hc + 1) * TT])
                            e.inc(sHL, 1)
                        for dc in range(DC):
                            kd = k * DC + dc
                            e.wait_ge(sY, kd + 1)
                            col = dc * S_LEN + tt * TT
                            if KNOACC:
                                e.tensor_copy(big[0:1, 0:TT] if False else hgt[:, 0:TT], pY[dc % 2])
                            else:
                                e.tensor_add(acc[:, col:col + TT], acc[:, col:col + TT],
                                             pY[dc % 2])
                            e.inc(sYE, 1)

            @blk.scalar
            def _(e):
                e = Eng(e)
                join(e, pn)
                for ee in range(NE_RUN):
                    for j in range(3):
                        m = 3 * ee + j
                        e.wait_ge(sWF, 16 * (m + 1))
                        if m >= 3:
                            if j == 2:
                                e.wait_ge(sY, DC * NT * ee)
                            else:
                                e.wait_ge(sH, 2 * NT * ee)
                        e.copy(out=wslot(wr_hi, m), in_=wtmp)
                        e.inc(sWHc, 1)
                    for tt in range(NT):
                        k = ee * NT + tt
                        for half in range(2):
                            kk = 2 * k + half
                            e.wait_ge(sH, kk + 1)
                            for c in range(2):
                                if KNOSILU:
                                    e.copy(out=p2(sil, c), in_=ph[c])
                                else:
                                    e.activation(p2(sil, c), ph[c], AF.Silu)
                                e.inc(sSIL, 1)
                            e.wait_ge(sHM, kk + 1)
                            for c in range(2):
                                hc = 2 * half + c
                                e.copy(out=hhi[:, hc * TT:(hc + 1) * TT], in_=p2(hgt, c))
                            e.inc(sHH, 1)

            @blk.tensor
            def _(e):
                e = Eng(e)
                join(e, pn)
                for ee in range(NE_RUN):
                    for tt in range(NT):
                        k = ee * NT + tt
                        e.wait_ge(sXT2[k % 2], 32 * (k // 2 + 1))
                        xh = xs(xth, k)
                        xl = xs(xtl, k)
                        if ee < 8:
                            if k >= 2:
                                e.wait_ge(sGBE, k - 1)
                            e.matmul(pGB[k % 2], inds_s[:, ee * 128:(ee + 1) * 128],
                                     scT[:, tt * TT:(tt + 1) * TT], start=True, stop=True)
                            e.inc(sGB, 1)
                        else:
                            e.nop().then_inc(sGB, 1)
                        e.wait_ge(sWL, 3 * ee + 2)
                        for half in range(2):
                            kk = 2 * k + half
                            if kk >= 1:
                                e.wait_ge(sHM, kk)
                            w1h, w1l = wslot(wr_hi, 3 * ee), wslot(wr_lo, 3 * ee)
                            w3h, w3l = wslot(wr_hi, 3 * ee + 1), wslot(wr_lo, 3 * ee + 1)
                            for c in range(2):
                                hc = 2 * half + c
                                for dc in range(DC):
                                    co = dc * TT + hc * 128
                                    mm3(e, ph[c],
                                        w1h[:, co:co + 128], w1l[:, co:co + 128],
                                        xh[:, dc * TT:(dc + 1) * TT],
                                        xl[:, dc * TT:(dc + 1) * TT],
                                        dc == 0, dc == DC - 1)
                            for c in range(2):
                                hc = 2 * half + c
                                for dc in range(DC):
                                    co = dc * TT + hc * 128
                                    mm3(e, ph[2 + c],
                                        w3h[:, co:co + 128], w3l[:, co:co + 128],
                                        xh[:, dc * TT:(dc + 1) * TT],
                                        xl[:, dc * TT:(dc + 1) * TT],
                                        dc == 0, dc == DC - 1)
                            e.inc(sH, 1)
                        e.wait_ge(sWL, 3 * ee + 3)
                        e.wait_ge(sHL, 2 * k + 2)
                        e.wait_ge(sGBE, k + 1)
                        w2h, w2l = wslot(wr_hi, 3 * ee + 2), wslot(wr_lo, 3 * ee + 2)
                        for dc in range(DC):
                            kd = k * DC + dc
                            if kd >= 2:
                                e.wait_ge(sYE, kd - 1)
                            for hc in range(HH):
                                co = hc * D + dc * 128
                                mm3(e, pY[dc % 2],
                                    w2h[:, co:co + 128], w2l[:, co:co + 128],
                                    hhi[:, hc * TT:(hc + 1) * TT],
                                    hlo[:, hc * TT:(hc + 1) * TT],
                                    hc == 0, hc == HH - 1)
                            e.inc(sY, 1)
                e.wait_ge(sYE, NE_RUN * NT * DC)

    def spill_acc(dst):
        with sem_t("sSPA") as s, nc.Block() as blk:
            ALLS = [s]
            pn = open_phase()
            @blk.sync
            def _(e):
                e = Eng(e)
                clears(e, ALLS, pn)
                e.dma_start(out=dst, in_=acc.rearrange("p (c t) -> p c t", c=DC)).then_inc(s, 16)
                e.wait_ge(s, 16)

    def attn_ln(b):
        with ExitStack() as st, nc.Block() as blk:
            sem = {n: st.enter_context(sem_t(n)) for n in
                   "sSQ sS1 sEp1 sEp2 sEp3 sBC sXF sXH sXN".split()}
            sX2 = [st.enter_context(sem_t("sX")) for _ in range(2)]
            sSP2 = [st.enter_context(sem_t("sSP")) for _ in range(2)]
            ALLS = list(sem.values()) + sX2 + sSP2
            pn = open_phase()
            sSQ, sS1 = sem["sSQ"], sem["sS1"]
            sEp1, sEp2, sEp3, sBC = sem["sEp1"], sem["sEp2"], sem["sEp3"], sem["sBC"]
            sXF, sXH, sXN = sem["sXF"], sem["sXH"], sem["sXN"]

            pLN1 = st.enter_context(psum_t("pLN1", [1, TT], F32)).ap()
            pLN2 = st.enter_context(psum_t("pLN2", [1, TT], F32)).ap()
            pB1 = st.enter_context(psum_t("pB1", [128, TT], F32)).ap()
            pB2 = st.enter_context(psum_t("pB2", [128, TT], F32)).ap()

            x1t = st.enter_context(sbuf_t("x1t", [128, 2 * DC * TT], F32)).ap()
            xnh = st.enter_context(sbuf_t("xnh", [128, 2 * DC * TT], BF16)).ap()
            xnl = st.enter_context(sbuf_t("xnl", [128, 2 * DC * TT], BF16)).ap()
            big = st.enter_context(sbuf_t("big", [128, DC * TT], F32)).ap()
            lnrow = st.enter_context(sbuf_t("lnrow", [1, 12 * TT], F32)).ap()

            def lnr(i):
                return lnrow[0:1, i * TT:(i + 1) * TT]

            def x1v(tt, dc):
                base = (tt % 2) * DC * TT
                return x1t[:, base + dc * TT: base + (dc + 1) * TT]

            def xnv(x, tt, dc):
                base = (tt % 2) * DC * TT
                return x[:, base + dc * TT: base + (dc + 1) * TT]

            def xslot(x, tt):
                return x[:, (tt % 2) * DC * TT:(tt % 2 + 1) * DC * TT]

            @blk.sync
            def _(e):
                e = Eng(e)
                clears(e, ALLS, pn)
                for tt in range(NT):
                    if tt >= 2:
                        e.wait_ge(sXN, tt - 1)
                    e.dma_start(
                        out=xslot(x1t, tt).rearrange("p (c t) -> p c t", c=DC),
                        in_=x1s[b][:, :, tt * TT:(tt + 1) * TT].rearrange("c p t -> p c t"),
                    ).then_inc(sX2[tt % 2], 16)

            @blk.vector
            def _(e):
                e = Eng(e, auto=True)
                join(e, pn)
                for tt in range(NT):
                    e.wait_ge(sX2[tt % 2], 16 * (tt // 2 + 1))
                    for dc in range(DC):
                        e.tensor_mul(big[:, dc * TT:(dc + 1) * TT], x1v(tt, dc), x1v(tt, dc))
                    e.inc(sSQ, 1)
                    e.wait_ge(sEp1, 1 + 2 * tt)
                    e.tensor_mul(lnr(2), lnr(0), lnr(0))
                    e.tensor_sub(lnr(3), lnr(1), lnr(2))
                    e.tensor_scalar_add(lnr(4), lnr(3), epsrow)
                    e.inc(sEp2, 1)
                    e.wait_ge(sEp1, 2 + 2 * tt)
                    e.reciprocal(lnr(6), lnr(5))
                    e.tensor_mul(lnr(7), lnr(6), lnr(6))
                    e.tensor_mul(lnr(8), lnr(7), lnr(4))
                    e.tensor_scalar(lnr(9), lnr(8), consts[0:1, 2:3], consts[0:1, 3:4],
                                    ALU.mult, ALU.add)
                    e.tensor_mul(lnr(10), lnr(6), lnr(9))
                    e.tensor_mul(lnr(11), lnr(0), lnr(10))
                    e.inc(sEp3, 1)
                    e.wait_ge(sBC, tt + 1)
                    if tt >= 2:
                        e.wait_ge(sSP2[tt % 2], 32 * (tt // 2))
                    for dc in range(DC):
                        e.tensor_mul(big[:, dc * TT:(dc + 1) * TT], x1v(tt, dc), pB1)
                        e.tensor_sub(big[:, dc * TT:(dc + 1) * TT],
                                     big[:, dc * TT:(dc + 1) * TT], pB2)
                    e.inc(sXF, 1)
                    e.wait_ge(sXH, tt + 1)
                    for dc in range(DC):
                        e.tensor_sub(xnv(xnl, tt, dc), big[:, dc * TT:(dc + 1) * TT],
                                     xnv(xnh, tt, dc))
                    e.inc(sXN, 1)

            @blk.scalar
            def _(e):
                e = Eng(e)
                join(e, pn)
                for tt in range(NT):
                    e.wait_ge(sS1, tt + 1)
                    e.activation(lnr(0), pLN1, AF.Copy, scale=1.0 / D)
                    e.activation(lnr(1), pLN2, AF.Copy, scale=1.0 / D)
                    e.inc(sEp1, 1)
                    e.wait_ge(sEp2, tt + 1)
                    e.activation(lnr(5), lnr(4), AF.Sqrt)
                    e.inc(sEp1, 1)
                    e.wait_ge(sXF, tt + 1)
                    for dc in range(DC):
                        e.copy(out=xnv(xnh, tt, dc), in_=big[:, dc * TT:(dc + 1) * TT])
                    e.inc(sXH, 1)
                    e.wait_ge(sXN, tt + 1)
                    e.dma_start(
                        out=xnhs.rearrange("c p t -> p c t")[:, :, tt * TT:(tt + 1) * TT],
                        in_=xslot(xnh, tt).rearrange("p (c t) -> p c t", c=DC),
                    ).then_inc(sSP2[tt % 2], 16)
                    e.dma_start(
                        out=xnls.rearrange("c p t -> p c t")[:, :, tt * TT:(tt + 1) * TT],
                        in_=xslot(xnl, tt).rearrange("p (c t) -> p c t", c=DC),
                    ).then_inc(sSP2[tt % 2], 16)

            @blk.tensor
            def _(e):
                e = Eng(e)
                join(e, pn)
                for tt in range(NT):
                    e.wait_ge(sX2[tt % 2], 16 * (tt // 2 + 1))
                    e.wait_ge(sSQ, tt + 1)
                    if tt >= 1:
                        e.wait_ge(sEp1, 2 * tt - 1)
                    for dc in range(DC):
                        e.matmul(pLN1, onescol, x1v(tt, dc),
                                 start=(dc == 0), stop=(dc == DC - 1))
                    for dc in range(DC):
                        e.matmul(pLN2, onescol, big[:, dc * TT:(dc + 1) * TT],
                                 start=(dc == 0), stop=(dc == DC - 1))
                    e.inc(sS1, 1)
                    e.wait_ge(sEp3, tt + 1)
                    if tt >= 1:
                        e.wait_ge(sXF, tt)
                    e.matmul(pB1, ones1, lnr(10), start=True, stop=True)
                    e.matmul(pB2, ones1, lnr(11), start=True, stop=True)
                    e.inc(sBC, 1)
                e.wait_ge(sSP2[0], 64)
                e.wait_ge(sSP2[1], 64)

    def attn_pair(pair):
        with ExitStack() as st:
            wqh = st.enter_context(sbuf_t("wqh", [128, DC * 128], BF16)).ap()
            wql = st.enter_context(sbuf_t("wql", [128, DC * 128], BF16)).ap()
            wkh = st.enter_context(sbuf_t("wkh", [128, DC * 128], BF16)).ap()
            wkl = st.enter_context(sbuf_t("wkl", [128, DC * 128], BF16)).ap()
            wvh = st.enter_context(sbuf_t("wvh", [128, DC * 128], BF16)).ap()
            wvl = st.enter_context(sbuf_t("wvl", [128, DC * 128], BF16)).ap()
            khi = st.enter_context(sbuf_t("khi", [128, S_LEN], BF16)).ap()
            klo = st.enter_context(sbuf_t("klo", [128, S_LEN], BF16)).ap()
            qhi = st.enter_context(sbuf_t("qhi", [128, S_LEN], BF16)).ap()
            qlo = st.enter_context(sbuf_t("qlo", [128, S_LEN], BF16)).ap()
            vhi = st.enter_context(sbuf_t("vhi", [128, 16 * 144], BF16)).ap()
            vlo = st.enter_context(sbuf_t("vlo", [128, 16 * 144], BF16)).ap()
            tabs = st.enter_context(sbuf_t("tabs", [128, 4 * S_LEN], F32)).ap()
            p128_s = st.enter_context(sbuf_t("p128_s", [128, 128], F32)).ap()

            def tab(i, tt):
                return tabs[:, i * S_LEN + tt * TT: i * S_LEN + (tt + 1) * TT]

            with ExitStack() as st2, nc.Block() as blk:
                sem = {n: st2.enter_context(sem_t(n)) for n in
                       "sWS sWHc sWLc sKQ sKC sPP sRO sRH sRL sVV sVH sVL".split()}
                sXT2 = [st2.enter_context(sem_t("sXT")) for _ in range(2)]
                ALLS = list(sem.values()) + sXT2
                pn = open_phase()
                sWS, sWHc, sWLc = sem["sWS"], sem["sWHc"], sem["sWLc"]
                sKQ, sKC, sPP, sRO = sem["sKQ"], sem["sKC"], sem["sPP"], sem["sRO"]
                sRH, sRL, sVV, sVH = sem["sRH"], sem["sRL"], sem["sVV"], sem["sVH"]
                sVL = sem["sVL"]

                pKQ = [st2.enter_context(psum_t(f"pKQ{i}", [128, TT], F32)).ap()
                       for i in range(2)]
                pPP = [st2.enter_context(psum_t(f"pPP{i}", [128, TT], F32)).ap()
                       for i in range(2)]
                pV = [st2.enter_context(psum_t(f"pV{i}", [128, 128], F32)).ap()
                      for i in range(2)]

                wsl = st2.enter_context(sbuf_t("wsl", [128, DC * 128], F32)).ap()
                xth = st2.enter_context(sbuf_t("xth", [128, 2 * DC * TT], BF16)).ap()
                xtl = st2.enter_context(sbuf_t("xtl", [128, 2 * DC * TT], BF16)).ap()
                kqb = st2.enter_context(sbuf_t("kqb", [128, TT], F32)).ap()
                rtm = st2.enter_context(sbuf_t("rtm", [128, 4 * TT], F32)).ap()

                whl = [(wkh, wkl), (wvh, wvl), (wqh, wql)]
                wsrc = [wkT, wvT, wqT]

                def xs(x, k):
                    return x[:, (k % 2) * DC * TT:(k % 2 + 1) * DC * TT]

                def p2(x, j):
                    return x[:, (j % 2) * TT:(j % 2 + 1) * TT]

                @blk.sync
                def _(e):
                    e = Eng(e)
                    clears(e, ALLS, pn)
                    e.dma_start(out=p128_s, in_=p128T).then_inc(sWS, 16)
                    for i, t in enumerate((tch, tcl, tsh, tsl)):
                        e.dma_start(out=tabs[:, i * S_LEN:(i + 1) * S_LEN], in_=t,
                                    ).then_inc(sWS, 16)
                    c0_ = pair * 128
                    for i in range(3):
                        if i >= 1:
                            e.wait_ge(sWLc, i)
                        e.dma_start(out=wsl.rearrange("p (c o) -> p c o", c=DC),
                                    in_=wsrc[i].rearrange("(c p) o -> p c o", p=128)[:, :, c0_:c0_ + 128],
                                    ).then_inc(sWS, 16)
                    for rnd in range(2):
                        for tt in range(NT):
                            k = rnd * NT + tt
                            if k >= 2:
                                prev = k - 2
                                if prev < 4:
                                    e.wait_ge(sVV, 4 * prev + 4)
                                else:
                                    e.wait_ge(sKQ, prev + 1)
                            e.dma_start(out=xs(xth, k).rearrange("p (c t) -> p c t", c=DC),
                                        in_=xnhs.rearrange("c p t -> p c t")[:, :, tt * TT:(tt + 1) * TT],
                                        ).then_inc(sXT2[k % 2], 16)
                            e.dma_start(out=xs(xtl, k).rearrange("p (c t) -> p c t", c=DC),
                                        in_=xnls.rearrange("c p t -> p c t")[:, :, tt * TT:(tt + 1) * TT],
                                        ).then_inc(sXT2[k % 2], 16)

                base = 80

                @blk.scalar
                def _(e):
                    e = Eng(e)
                    join(e, pn)
                    for i in range(3):
                        e.wait_ge(sWS, base + 16 * (i + 1))
                        e.copy(out=whl[i][0], in_=wsl)
                        e.inc(sWHc, 1)
                    for et in range(8):
                        tt = et % NT
                        e.wait_ge(sRO, et + 1)
                        dsth = khi if et < 4 else qhi
                        e.copy(out=dsth[:, tt * TT:(tt + 1) * TT], in_=rtm[:, 0:TT])
                        e.inc(sRH, 1)
                        if et < 4:
                            for tc in range(4):
                                vt = 4 * tt + tc
                                e.wait_ge(sVV, vt + 1)
                                uc = vt
                                e.copy(out=vhi[:, uc * 144:uc * 144 + 64],
                                       in_=pV[vt % 2][:, 0:64])
                                e.copy(out=vhi[:, uc * 144 + 72:uc * 144 + 136],
                                       in_=pV[vt % 2][:, 64:128])
                                e.inc(sVH, 1)

                @blk.vector
                def _(e):
                    e = Eng(e, auto=True)
                    join(e, pn)
                    e.memset(vlo, 0.0)
                    for uc in range(16):
                        e.memset(vhi[:, uc * 144 + 64:uc * 144 + 65], 1.0)
                        e.memset(vhi[:, uc * 144 + 136:uc * 144 + 137], 1.0)
                    for i in range(3):
                        e.wait_ge(sWHc, i + 1)
                        e.tensor_sub(whl[i][1], wsl, whl[i][0])
                        e.inc(sWLc, 1)
                    for et in range(8):
                        tt = et % NT
                        e.wait_ge(sKQ, et + 1)
                        if et >= 1:
                            e.wait_ge(sPP, et)
                        e.tensor_copy(kqb, pKQ[et % 2])
                        e.inc(sKC, 1)
                        e.wait_ge(sPP, et + 1)
                        e.tensor_mul(rtm[:, 0:TT], kqb, tab(0, tt))
                        e.tensor_mul(rtm[:, TT:2 * TT], kqb, tab(1, tt))
                        e.tensor_add(rtm[:, 2 * TT:3 * TT], rtm[:, 0:TT], rtm[:, TT:2 * TT])
                        e.tensor_mul(rtm[:, 0:TT], pPP[et % 2], tab(2, tt))
                        e.tensor_mul(rtm[:, TT:2 * TT], pPP[et % 2], tab(3, tt))
                        e.tensor_add(rtm[:, 3 * TT:4 * TT], rtm[:, 0:TT], rtm[:, TT:2 * TT])
                        e.tensor_add(rtm[:, 0:TT], rtm[:, 2 * TT:3 * TT], rtm[:, 3 * TT:4 * TT])
                        e.inc(sRO, 1)
                        e.wait_ge(sRH, et + 1)
                        dsth = khi if et < 4 else qhi
                        dstl = klo if et < 4 else qlo
                        e.tensor_sub(dstl[:, tt * TT:(tt + 1) * TT], rtm[:, 0:TT],
                                     dsth[:, tt * TT:(tt + 1) * TT])
                        e.inc(sRL, 1)
                        if et < 4:
                            for tc in range(4):
                                vt = 4 * tt + tc
                                uc = vt
                                e.wait_ge(sVH, vt + 1)
                                e.tensor_sub(vlo[:, uc * 144:uc * 144 + 64],
                                             pV[vt % 2][:, 0:64],
                                             vhi[:, uc * 144:uc * 144 + 64])
                                e.tensor_sub(vlo[:, uc * 144 + 72:uc * 144 + 136],
                                             pV[vt % 2][:, 64:128],
                                             vhi[:, uc * 144 + 72:uc * 144 + 136])
                                e.inc(sVL, 1)

                @blk.tensor
                def _(e):
                    e = Eng(e)
                    join(e, pn)
                    for et in range(8):
                        rnd, tt = divmod(et, NT)
                        widx = 0 if rnd == 0 else 2
                        wh, wl = whl[widx]
                        e.wait_ge(sXT2[et % 2], 32 * (et // 2 + 1))
                        e.wait_ge(sWLc, widx + 1)
                        xh, xl = xs(xth, et), xs(xtl, et)
                        if et >= 2:
                            e.wait_ge(sKC, et - 1)
                        for dc in range(DC):
                            mm3(e, pKQ[et % 2],
                                wh[:, dc * 128:(dc + 1) * 128], wl[:, dc * 128:(dc + 1) * 128],
                                xh[:, dc * TT:(dc + 1) * TT], xl[:, dc * TT:(dc + 1) * TT],
                                dc == 0, dc == DC - 1)
                        e.inc(sKQ, 1)
                        e.wait_ge(sKC, et + 1)
                        if et >= 2:
                            e.wait_ge(sRO, et - 1)
                        e.matmul(pPP[et % 2], p128_s, kqb, start=True, stop=True)
                        e.inc(sPP, 1)
                        if rnd == 0:
                            e.wait_ge(sWLc, 2)
                            for tc in range(4):
                                vt = 4 * tt + tc
                                if vt >= 2:
                                    e.wait_ge(sVL, vt - 1)
                                for dc in range(DC):
                                    mm3(e, pV[vt % 2],
                                        xh[:, dc * TT + tc * 128: dc * TT + tc * 128 + 128],
                                        xl[:, dc * TT + tc * 128: dc * TT + tc * 128 + 128],
                                        wvh[:, dc * 128:(dc + 1) * 128],
                                        wvl[:, dc * 128:(dc + 1) * 128],
                                        dc == 0, dc == DC - 1)
                                e.inc(sVV, 1)
                    e.wait_ge(sRL, 8)
                    e.wait_ge(sVL, 16)

            with ExitStack() as st2, nc.Block() as blk:
                sem = {n: st2.enter_context(sem_t(n)) for n in
                       "sSC sMX sEX sE2 sEH sEL sTR sTC sAV sAO sAH sAL sSPL".split()}
                ALLS = list(sem.values())
                pn = open_phase()
                sSC, sMX, sEX, sEH = sem["sSC"], sem["sMX"], sem["sEX"], sem["sEH"]
                sE2 = sem["sE2"]
                sEL, sTR, sTC, sAV = sem["sEL"], sem["sTR"], sem["sTC"], sem["sAV"]
                sAO, sAH = sem["sAO"], sem["sAH"]
                sAL, sSPL = sem["sAL"], sem["sSPL"]

                pSC = [st2.enter_context(psum_t(f"pSC{i}", [128, TT], F32)).ap()
                       for i in range(4)]
                pTR = [st2.enter_context(psum_t(f"pTR{i}", [128, 128], BF16)).ap()
                       for i in range(2)]
                pAV = [st2.enter_context(psum_t(f"pAV{i}", [65, 128], F32)).ap()
                       for i in range(2)]

                aoh = st2.enter_context(sbuf_t("aoh", [128, S_LEN], BF16)).ap()
                aol = st2.enter_context(sbuf_t("aol", [128, S_LEN], BF16)).ap()
                ebuf = st2.enter_context(sbuf_t("ebuf", [128, S_LEN], F32)).ap()
                ehi = st2.enter_context(sbuf_t("ehi", [128, S_LEN], BF16)).ap()
                elo = st2.enter_context(sbuf_t("elo", [128, S_LEN], BF16)).ap()
                pth = st2.enter_context(sbuf_t("pth", [128, 2 * S_LEN], BF16)).ap()
                ptl = st2.enter_context(sbuf_t("ptl", [128, 2 * S_LEN], BF16)).ap()
                mneg = st2.enter_context(sbuf_t("mneg", [128, 8], F32)).ap()
                ebuf2 = st2.enter_context(sbuf_t("ebuf2", [128, S_LEN], F32)).ap()
                aot = st2.enter_context(sbuf_t("aot", [64, 256], F32)).ap()
                aoh0 = st2.enter_context(sbuf_t("aoh0", [64, 256], BF16)).ap()
                aol0 = st2.enter_context(sbuf_t("aol0", [64, 256], BF16)).ap()

                NGI = 32
                NGG = 16

                def pt_s(x, sb2, uc):
                    return x[:, sb2 * S_LEN + uc * 128: sb2 * S_LEN + (uc + 1) * 128]

                def emit_tr(e, gp):
                    for j in range(32):
                        src = ehi if j < 16 else elo
                        uc = j % 16
                        e.wait_ge(sEH if j < 16 else sEL, gp + 1)
                        B = 32 * gp + j
                        if B >= 2:
                            e.wait_ge(sTC, B - 1)
                        e.transpose(pTR[B % 2], src[:, uc * 128:(uc + 1) * 128],
                                    identb_s)
                        e.inc(sTR, 1)

                def emit_av(e, h, sg):
                    gg = h * 8 + sg
                    gi_b = h * 16 + 2 * sg + 1
                    e.wait_ge(sTC, 32 * (gi_b + 1))
                    if gg >= 1:
                        e.wait_ge(sAO, gg)
                    for uc in range(16):
                        va = vhi[:, uc * 144 + h * 72: uc * 144 + h * 72 + 65]
                        vb = vlo[:, uc * 144 + h * 72: uc * 144 + h * 72 + 65]
                        for sb2 in range(2):
                            out = pAV[sb2]
                            e.matmul(out, va, pt_s(pth, sb2, uc),
                                     start=(uc == 0), stop=False)
                            e.matmul(out, va, pt_s(ptl, sb2, uc), start=False, stop=False)
                            e.matmul(out, vb, pt_s(pth, sb2, uc), start=False,
                                     stop=(uc == 15))
                    e.inc(sAV, 1)

                def emit_trcopies(e, gp):
                    sb2p = (gp % NSB) % 2
                    for j in range(32):
                        B = 32 * gp + j
                        uc = j % 16
                        e.wait_ge(sTR, B + 1)
                        dst = pth if j < 16 else ptl
                        e.tensor_copy(pt_s(dst, sb2p, uc), pTR[B % 2])
                        e.inc(sTC, 1)

                def vector_av_tail(e, gg):
                    h, sg = divmod(gg, 8)
                    r0 = h * 64
                    e.wait_ge(sAV, gg + 1)
                    e.tensor_copy(aot[:, 0:128], pAV[0][0:64, :])
                    e.tensor_copy(aot[:, 128:256], pAV[1][0:64, :])
                    e.inc(sAO, 1)
                    e.wait_ge(sAH, gg + 1)
                    e.tensor_sub(aol0, aot, aoh0)
                    e.tensor_copy(aoh[r0:r0 + 64, sg * 256:(sg + 1) * 256], aoh0)
                    e.tensor_copy(aol[r0:r0 + 64, sg * 256:(sg + 1) * 256], aol0)
                    e.inc(sAL, 1)

                def scalar_ao(e, gg):
                    e.wait_ge(sAO, gg + 1)
                    e.copy(out=aoh0, in_=aot)
                    e.inc(sAH, 1)

                def group_done(gi):
                    if gi % 2 == 1:
                        return (gi // NSB) * 8 + (gi % NSB) // 2
                    return None

                @blk.tensor
                def _(e):
                    e = Eng(e)
                    clears(e, ALLS, pn)
                    for gi in range(NGI):
                        h, sb = divmod(gi, NSB)
                        r0 = h * 64
                        if gi >= 1:
                            e.wait_ge(sEX, 4 * gi)
                        for ut in range(4):
                            lq_h = qhi[r0:r0 + 64, sb * 128:(sb + 1) * 128]
                            lq_l = qlo[r0:r0 + 64, sb * 128:(sb + 1) * 128]
                            mm3(e, pSC[ut],
                                lq_h, lq_l,
                                khi[r0:r0 + 64, ut * TT:(ut + 1) * TT],
                                klo[r0:r0 + 64, ut * TT:(ut + 1) * TT],
                                True, True)
                            e.inc(sSC, 1)
                        if gi >= 1:
                            emit_tr(e, gi - 1)
                            gg = group_done(gi - 1)
                            if gg is not None:
                                emit_av(e, gg // 8, gg % 8)
                        if gi == NGI - 1:
                            emit_tr(e, gi)
                            emit_av(e, 1, 7)
                    e.wait_ge(sAL, NGG)

                @blk.vector
                def _(e):
                    e = Eng(e, auto=True)
                    join(e, pn)
                    for gi in range(NGI):
                        e.wait_ge(sSC, 4 * gi + 4)
                        for ut in range(4):
                            e.reduce_max(mneg[:, 1 + ut:2 + ut], pSC[ut], axis=AX.X)
                        e.reduce_max(mneg[:, 0:1], mneg[:, 1:5], axis=AX.X, negate=True)
                        e.inc(sMX, 1)
                        if gi >= 1:
                            emit_trcopies(e, gi - 1)
                        e.wait_ge(sEX, 4 * gi + 4)
                        e.reduce_sum(mneg[:, 5:6], ebuf, axis=AX.X)
                        e.drain()
                        e.reciprocal(mneg[:, 6:7], mneg[:, 5:6])
                        e.drain()
                        e.tensor_scalar(ebuf2, ebuf, mneg[:, 6:7], None, ALU.mult)
                        e.inc(sE2, 1)
                        e.wait_ge(sEH, gi + 1)
                        if gi >= 1:
                            e.wait_ge(sTR, 32 * gi)
                        e.tensor_sub(elo, ebuf2, ehi)
                        e.inc(sEL, 1)
                        if gi >= 1:
                            gg = group_done(gi - 1)
                            if gg is not None:
                                vector_av_tail(e, gg)
                        if gi == NGI - 1:
                            emit_trcopies(e, gi)
                            vector_av_tail(e, 15)

                @blk.scalar
                def _(e):
                    e = Eng(e)
                    join(e, pn)
                    for gi in range(NGI):
                        e.wait_ge(sMX, gi + 1)
                        if gi >= 1:
                            e.wait_ge(sE2, gi)
                        for ut in range(4):
                            e.activation(ebuf[:, ut * TT:(ut + 1) * TT],
                                         pSC[ut], AF.Exp, bias=mneg[:, 0:1])
                            e.inc(sEX, 1)
                        e.wait_ge(sE2, gi + 1)
                        if gi >= 1:
                            e.wait_ge(sTR, 32 * gi)
                        e.copy(out=ehi, in_=ebuf2)
                        e.inc(sEH, 1)
                        gg = group_done(gi - 1) if gi >= 1 else None
                        if gg is not None:
                            scalar_ao(e, gg)
                        if gi == NGI - 1:
                            scalar_ao(e, 15)
                    e.wait_ge(sAL, NGG)
                    e.dma_start(out=aohs[pair], in_=aoh).then_inc(sSPL, 16)
                    e.dma_start(out=aols[pair], in_=aol).then_inc(sSPL, 16)
                    e.wait_ge(sSPL, 32)

    def oproj(b):
        with ExitStack() as st, nc.Block() as blk:
            sem = {n: st.enter_context(sem_t(n)) for n in
                   "sWO sWOH sWOL sOC sOE".split()}
            sX12 = [st.enter_context(sem_t("sX1")) for _ in range(2)]
            sAT2 = [st.enter_context(sem_t("sAT")) for _ in range(2)]
            ALLS = list(sem.values()) + sX12 + sAT2
            pn = open_phase()
            sWO, sWOH, sWOL = sem["sWO"], sem["sWOH"], sem["sWOL"]
            sOC, sOE = sem["sOC"], sem["sOE"]

            pO = [st.enter_context(psum_t(f"pO{i}", [128, TT], F32)).ap()
                  for i in range(2)]

            big = st.enter_context(sbuf_t("big", [128, DC * TT], F32)).ap()
            woh = st.enter_context(sbuf_t("woh", [128, DC * D], BF16)).ap()
            wol = st.enter_context(sbuf_t("wol", [128, DC * D], BF16)).ap()
            x1b = st.enter_context(sbuf_t("x1b", [128, 2 * DC * TT], F32)).ap()
            ath = st.enter_context(sbuf_t("ath", [128, 2 * DC * TT], BF16)).ap()
            atl = st.enter_context(sbuf_t("atl", [128, 2 * DC * TT], BF16)).ap()

            def xs(x, k):
                return x[:, (k % 2) * DC * TT:(k % 2 + 1) * DC * TT]

            @blk.sync
            def _(e):
                e = Eng(e)
                clears(e, ALLS, pn)
                for hf in range(2):
                    if hf == 1:
                        e.wait_ge(sWOL, 1)
                    e.dma_start(
                        out=big.rearrange("p (c o) -> p c o", c=4),
                        in_=woT.rearrange("(c p) o -> p c o", p=128)[:, hf * 4:(hf + 1) * 4, :],
                    ).then_inc(sWO, 16)
                for tt in range(NT):
                    if tt >= 2:
                        e.wait_ge(sOE, 8 * (tt - 1))
                    e.dma_start(out=xs(x1b, tt).rearrange("p (c t) -> p c t", c=DC),
                                in_=x1s[b][:, :, tt * TT:(tt + 1) * TT].rearrange("c p t -> p c t"),
                                ).then_inc(sX12[tt % 2], 16)
                    for pr in range(NPAIR):
                        e.dma_start(out=xs(ath, tt)[:, pr * TT:(pr + 1) * TT],
                                    in_=aohs[pr][:, tt * TT:(tt + 1) * TT]).then_inc(sAT2[tt % 2], 16)
                        e.dma_start(out=xs(atl, tt)[:, pr * TT:(pr + 1) * TT],
                                    in_=aols[pr][:, tt * TT:(tt + 1) * TT]).then_inc(sAT2[tt % 2], 16)

            @blk.scalar
            def _(e):
                e = Eng(e)
                join(e, pn)
                for hf in range(2):
                    e.wait_ge(sWO, 16 * (hf + 1))
                    e.copy(out=woh[:, hf * DC * TT:(hf + 1) * DC * TT], in_=big)
                    e.inc(sWOH, 1)

            @blk.vector
            def _(e):
                e = Eng(e, auto=True)
                join(e, pn)
                for hf in range(2):
                    e.wait_ge(sWOH, hf + 1)
                    e.tensor_sub(wol[:, hf * DC * TT:(hf + 1) * DC * TT], big,
                                 woh[:, hf * DC * TT:(hf + 1) * DC * TT])
                    e.inc(sWOL, 1)
                for tt in range(NT):
                    for oc in range(DC):
                        j = tt * DC + oc
                        e.wait_ge(sOC, j + 1)
                        col = oc * S_LEN + tt * TT
                        e.tensor_add(acc[:, col:col + TT],
                                     xs(x1b, tt)[:, oc * TT:(oc + 1) * TT],
                                     pO[j % 2])
                        e.inc(sOE, 1)

            @blk.tensor
            def _(e):
                e = Eng(e)
                join(e, pn)
                e.wait_ge(sWOL, 2)
                for tt in range(NT):
                    e.wait_ge(sX12[tt % 2], 16 * (tt // 2 + 1))
                    e.wait_ge(sAT2[tt % 2], 256 * (tt // 2 + 1))
                    for oc in range(DC):
                        j = tt * DC + oc
                        if j >= 2:
                            e.wait_ge(sOE, j - 1)
                        for hc in range(DC):
                            co = hc * D + oc * 128
                            mm3(e, pO[j % 2],
                                woh[:, co:co + 128], wol[:, co:co + 128],
                                xs(ath, tt)[:, hc * TT:(hc + 1) * TT],
                                xs(atl, tt)[:, hc * TT:(hc + 1) * TT],
                                hc == 0, hc == DC - 1)
                        e.inc(sOC, 1)
                e.wait_ge(sOE, NT * DC)

    out_sem_cm = sem_t("sOUT")
    sOUT = out_sem_cm.__enter__()

    n_stores = [0]

    def store_out(b):
        n_stores[0] += 1
        with nc.Block() as blk:
            @blk.sync
            def _(e):
                e = Eng(e)
                e.dma_start(out=y[b].rearrange("(c p) t -> p c t", p=128),
                            in_=acc.rearrange("p (c t) -> p c t", c=DC)).then_inc(sOUT, 16)

    step = itertools.count()

    def go():
        return next(step) < limit

    for b in range(NB):
        if go():
            ffn_stageA(xT[b])
        if go():
            ffn_stageB()
        if go():
            spill_acc(x1s[b].rearrange("c p t -> p c t"))
    for b in range(NB):
        if go():
            attn_ln(b)
        for pair in range(NPAIR):
            if go():
                attn_pair(pair)
        if go():
            oproj(b)
        if go():
            ffn_stageA(None)
        if go():
            ffn_stageB()
        if go():
            store_out(b)

    with nc.Block() as blk:
        @blk.sync
        def _(e):
            e = Eng(e)
            if n_stores[0]:
                e.wait_ge(sOUT, 16 * n_stores[0])
    out_sem_cm.__exit__(None, None, None)
    gP_cm.__exit__(None, None, None)
    return nc



_prog_cache = {}


def _tables():
    hd = 64
    inv = (1.0 / (np.float32(10000.0) ** (np.arange(0, hd, 2, dtype=np.float32) / np.float32(hd)))).astype(np.float32)
    t = np.arange(S_LEN, dtype=np.float32)
    freqs = (t[:, None] * inv[None, :]).astype(np.float32)
    emb = np.concatenate([freqs, freqs], axis=1)
    s8 = np.sqrt(np.float64(8.0))
    c64 = np.cos(emb.astype(np.float64)) * s8
    s64 = np.sin(emb.astype(np.float64)) * s8
    ch = c64.astype(np.float32)
    cl = (c64 - ch.astype(np.float64)).astype(np.float32)
    sh = s64.astype(np.float32)
    sl = (s64 - sh.astype(np.float64)).astype(np.float32)
    def dup(a):
        aT = np.ascontiguousarray(a.T)
        return np.ascontiguousarray(np.concatenate([aT, aT], axis=0))
    P = np.zeros((64, 64), dtype=np.float32)
    for i in range(32):
        P[i, 2 * i + 1] = -1.0
        P[32 + i, 2 * i] = 1.0
    P128 = np.zeros((128, 128), dtype=np.float32)
    P128[0:64, 0:64] = P
    P128[64:128, 64:128] = P
    return dup(ch), dup(cl), dup(sh), dup(sl), np.ascontiguousarray(P128.T)


def _indicator():
    a = np.zeros((8, 8 * 128), dtype=np.float32)
    for e in range(8):
        a[e, e * 128:(e + 1) * 128] = 1.0
    return a


def kernel(L0_attn_hidden_states, L0_attn_Wq, L0_attn_Wk, L0_attn_Wv, L0_attn_Wo,
           L0_ffn_router_w, L0_ffn_router_b, L0_ffn_routed_w1, L0_ffn_routed_w3,
           L0_ffn_routed_w2, L0_ffn_shared_w1, L0_ffn_shared_w3, L0_ffn_shared_w2):
    x = np.asarray(L0_attn_hidden_states, dtype=np.float32)
    BTOT = x.shape[0]
    ncores = 8
    xTfull = np.ascontiguousarray(x.transpose(0, 2, 1))

    ch, cl, sh, sl, p128T = _tables()
    common = {
        "wqT": np.ascontiguousarray(np.asarray(L0_attn_Wq, np.float32).T),
        "wkT": np.ascontiguousarray(np.asarray(L0_attn_Wk, np.float32).T),
        "wvT": np.ascontiguousarray(np.asarray(L0_attn_Wv, np.float32).T),
        "woT": np.ascontiguousarray(np.asarray(L0_attn_Wo, np.float32).T),
        "w1": np.ascontiguousarray(np.concatenate([L0_ffn_routed_w1, L0_ffn_shared_w1], 0).astype(np.float32)),
        "w3": np.ascontiguousarray(np.concatenate([L0_ffn_routed_w3, L0_ffn_shared_w3], 0).astype(np.float32)),
        "w2": np.ascontiguousarray(np.concatenate([L0_ffn_routed_w2, L0_ffn_shared_w2], 0).astype(np.float32)),
        "rw": np.ascontiguousarray(np.pad(np.asarray(L0_ffn_router_w, np.float32),
                                           ((0, 0), (0, 120)))),
        "rbias": np.ascontiguousarray(np.asarray(L0_ffn_router_b, np.float32)).reshape(1, 8),
        "tch": ch, "tcl": cl, "tsh": sh, "tsl": sl, "p128T": p128T,
        "ident_b": np.eye(128, dtype=np.float32).astype(ml_dtypes.bfloat16),
        "ident_f": np.eye(128, dtype=np.float32),
        "inds": _indicator(),
    }

    if "nc" not in _prog_cache:
        _prog_cache["nc"] = build_program()
    nc = _prog_cache["nc"]

    in_maps = []
    for c in range(ncores):
        m = dict(common)
        m["xT"] = np.ascontiguousarray(xTfull[NB * c:NB * (c + 1)])
        in_maps.append(m)

    res = run_bass_kernel_spmd(nc, in_maps, core_ids=list(range(ncores)))
    out = np.empty((BTOT, S_LEN, D), dtype=np.float32)
    for c in range(ncores):
        yc = res.results[c]["y"]
        for j in range(NB):
            out[NB * c + j] = yc[j].T
    return out
